# version 25
# baseline (speedup 1.0000x reference)
import numpy as np
import ml_dtypes
from contextlib import ExitStack

import concourse.bass as bass
import concourse.mybir as mybir
from concourse.bass_utils import run_bass_kernel_spmd

F32 = mybir.dt.float32
BF16 = mybir.dt.bfloat16
AF = mybir.ActivationFunctionType
ALU = mybir.AluOpType
AX = mybir.AxisListType

NEG = -1.0e30
B, S, DM = 4, 4096, 1024
NH, NKV, HD = 16, 4, 64
WINDOW = 1024
QTOK = 2048
CTX = 3072
DCH = DM // 128
NQT = QTOK // 128
NREL = 9
CTXCH = CTX // 128
DFF = 4096
NFC = DFF // 128
EPS = 1e-5
QCNT = float(DM * DFF)

ENGS = ("tensor", "vector", "scalar", "sync", "gpsimd")


class Sched:

    SEM_NAMES = ("T", "V", "A", "G", "BAR", "LDS", "GW", "XB0", "XB1",
                 "QI0", "QI1", "QI2", "QI3", "QO0", "QO1", "W10", "W11", "W12",
                 "W20", "W21", "W22", "OST", "XR0", "XR1", "XR2", "XR3", "R1S0", "R1S1",
                 "RL", "RLG", "MRL", "MRS")

    def __init__(self, nc, es):
        self.nc = nc
        self.sems = {}
        self.c = {}
        self.events = []
        for name in self.SEM_NAMES:
            self.sems[name] = es.enter_context(nc.semaphore(name))
            self.c[name] = 0

    AUTO_SELF = {"vector": "V", "scalar": "A", "gpsimd": "G"}

    def do(self, eng, fn=None, waits=(), incs=()):
        sem = self.AUTO_SELF.get(eng)
        if sem is not None:
            if self.c[sem] > 0:
                waits = ((sem, self.c[sem]),) + tuple(waits)
            if not any(s == sem for s, _ in incs):
                incs = tuple(incs) + ((sem, 1),)
        for s, v in waits:
            assert v <= self.c[s], f"wait {s}>={v} but only {self.c[s]} emitted"
        self.events.append((eng, tuple(waits), fn, tuple(incs)))
        for s, n in incs:
            self.c[s] += n

    def dma(self, qeng, dst, src, sem, waits=()):
        w = tuple(waits)
        if self.c[sem] > 0:
            w = ((sem, self.c[sem]),) + w
        self.do(qeng, lambda e, d=dst, s=src: e.dma_start(d, s), waits=w,
                incs=((sem, 16),))
        return self.c[sem]

    def barrier(self, drain=()):
        for e in ENGS:
            self.do(e, None, incs=(("BAR", 1),))
        tgt = self.c["BAR"]
        dw = tuple((s, self.c[s]) for s in drain)
        for e in ENGS:
            self.do(e, None, waits=(("BAR", tgt),) + dw)

    def flush(self):
        evs, self.events = self.events, []
        if not evs:
            return
        with self.nc.Block() as blk:
            for name in ENGS:
                mine = [ev for ev in evs if ev[0] == name]
                if not mine:
                    continue

                def mk(mine=mine):
                    def f(eng):
                        for _, waits, fn, incs in mine:
                            for s, v in waits:
                                eng.wait_ge(self.sems[s], v)
                            ins = fn(eng) if fn is not None else None
                            used = False
                            for s, n in incs:
                                if ins is not None and not used:
                                    ins.then_inc(self.sems[s], n)
                                    used = True
                                else:
                                    eng.sem_inc(self.sems[s], n)
                    return f

                getattr(blk, name)(mk())


def build_nc(stop_after=None, p2_level=4, q2_on=True, p2_ng=None, skip_exp=False, p2_pieces=None):
    nc = bass.Bass("TRN2", target_bir_lowering=False)

    xctxT = nc.dram_tensor("xctxT", [DM, CTX], F32, kind="ExternalInput").ap()
    wqT = nc.dram_tensor("wqT", [DM, DM], F32, kind="ExternalInput").ap()
    wkTd = nc.dram_tensor("wkTd", [DM, 512], F32, kind="ExternalInput").ap()
    wvT = nc.dram_tensor("wvT", [DM, 256], F32, kind="ExternalInput").ap()
    woT = nc.dram_tensor("woT", [DM, DM], F32, kind="ExternalInput").ap()
    w1T = nc.dram_tensor("w1T", [DM, DFF], F32, kind="ExternalInput").ap()
    w2T = nc.dram_tensor("w2T", [DFF, DM], F32, kind="ExternalInput").ap()
    b1c = nc.dram_tensor("b1c", [128, NFC], F32, kind="ExternalInput").ap()
    b2c = nc.dram_tensor("b2c", [128, DCH], F32, kind="ExternalInput").ap()
    trimask = nc.dram_tensor("trimask", [128, 1024], BF16, kind="ExternalInput").ap()
    identin = nc.dram_tensor("identin", [128, 128], BF16, kind="ExternalInput").ap()
    validrep = nc.dram_tensor("validrep", [CTX, 128], BF16, kind="ExternalInput").ap()
    outT = nc.dram_tensor("outT", [DM, QTOK], F32, kind="ExternalOutput").ap()
    w1q_t = nc.dram_tensor("w1q_t", [NFC, 128, DM], BF16).ap()
    r1d = nc.dram_tensor("r1d", [DM, QTOK], F32).ap()
    w2q_t = nc.dram_tensor("w2q_t", [DCH, 128, DFF], BF16).ap()
    meand = nc.dram_tensor("meand", [128, QTOK], F32).ap()
    rstdd = nc.dram_tensor("rstdd", [128, QTOK], F32).ap()

    es = ExitStack()
    with es:
        S = Sched(nc, es)
        sb = lambda name, shape, dt: es.enter_context(nc.sbuf_tensor(name, shape, dt))

        tri_sb = sb("tri_sb", [128, 1024], BF16)
        id_sb = sb("id_sb", [128, 128], BF16)
        b1_sb = sb("b1_sb", [128, NFC], F32)
        b2_sb = sb("b2_sb", [128, DCH], F32)
        ones_bf = sb("ones_bf", [128, 128], BF16)
        ones_f32 = sb("ones_f32", [128, 128], F32)
        qred = sb("qred", [128, 512], F32)
        gpart = sb("gpart", [128, 32], F32)
        gsum = sb("gsum", [128, 2], F32)
        th1 = sb("th1", [128, 1], F32)
        nth1 = sb("nth1", [128, 1], F32)
        th2 = sb("th2", [128, 1], F32)
        nth2 = sb("nth2", [128, 1], F32)
        g1col = sb("g1col", [128, 1], F32)
        g2col = sb("g2col", [128, 1], F32)
        cm30 = sb("cm30", [128, 1], F32)

        w1b = [es.enter_context(nc.sbuf_tensor(f"w1b{i}", [128, DM], BF16,
                                               side="right")) for i in range(3)]
        otn_ctx = ExitStack()
        OTn = [otn_ctx.enter_context(
            nc.sbuf_tensor(f"OTn{i}", [128, DCH * 512], BF16, side="right"))
            for i in range(2)]
        wo_bf = otn_ctx.enter_context(
            nc.sbuf_tensor("wo_bf", [128, DCH * 1024], BF16, side="right"))

        ld_small = 0
        gd_all = 0

        def _early_out(src_ap):
            v = S.dma("sync", outT.rearrange("(dc p) q -> p dc q", p=128)[:, 0, 0:512],
                      src_ap, "OST")
            S.do("sync", None, waits=(("OST", v),))

        with ExitStack() as p12:
            sb12 = lambda name, shape, dt: p12.enter_context(nc.sbuf_tensor(name, shape, dt))
            KTd = sb12("KTd", [128, 4 * CTX], BF16)
            Vv = sb12("Vv", [128, CTXCH * 256], BF16)
            QT = sb12("QT", [128, DCH * QTOK], BF16)
            vrep_sb = sb12("vrep_sb", [128, CTXCH * 128], BF16)
            qin = [sb12(f"qin{i}", [128, 1024], F32) for i in range(4)]
            qx = sb12("qx", [128, 1024], BF16)
            qout = [sb12(f"qout{i}", [128, 1024], BF16) for i in range(2)]
            mean_all = sb12("mean_all", [128, QTOK], F32)
            rstd_all = sb12("rstd_all", [128, QTOK], F32)

            with ExitStack() as p1:
                sb1 = lambda name, shape, dt: p1.enter_context(nc.sbuf_tensor(name, shape, dt))
                ps1 = lambda name: p1.enter_context(nc.psum_tensor(name, [128, 512], F32))
                xb = [sb1(f"xb{i}", [128, DCH * 512], BF16) for i in range(2)]
                wq_bf = sb1("wq_bf", [128, DCH * 1024], BF16)
                wkd_bf = sb1("wkd_bf", [128, DCH * 512], BF16)
                wv_bf = sb1("wv_bf", [128, DCH * 256], BF16)
                kps = [ps1(f"kps{i}") for i in range(2)]
                vps = [ps1(f"vps{i}") for i in range(2)]
                qps = [ps1(f"qps{i}") for i in range(2)]

                S.dma("sync", tri_sb[:, :], trimask, "LDS")
                S.dma("sync", id_sb[:, :], identin, "LDS")
                S.dma("sync", b1_sb[:, :], b1c, "LDS")
                S.dma("sync", b2_sb[:, :], b2c, "LDS")
                S.dma("sync", vrep_sb[:].rearrange("p (ca j) -> p ca j", ca=CTXCH),
                      validrep.rearrange("(ca p) j -> p ca j", p=128), "LDS")
                ld_small = S.c["LDS"]
                S.do("vector", lambda e: e.memset(ones_bf[:, :], 1.0), incs=(("V", 1),))
                S.do("vector", lambda e: e.memset(ones_f32[:, :], 1.0), incs=(("V", 1),))
                S.do("vector", lambda e: e.memset(cm30[:, :], -30.0), incs=(("V", 1),))
                v_consts = S.c["V"]

                xctxT_r = xctxT.rearrange("(dc p) c -> p dc c", p=128)

                def emit_xb_dma(cb, wait_t):
                    dst = xb[cb % 2][:].rearrange("p (dc c) -> p dc c", dc=DCH)
                    src = xctxT_r[:, :, cb * 512:(cb + 1) * 512]
                    waits = (("T", wait_t),) if wait_t > 0 else ()
                    return S.dma("gpsimd", dst, src, f"XB{cb % 2}", waits=waits)

                gw_k = S.dma("gpsimd", wkd_bf[:].rearrange("p (dc f) -> p dc f", dc=DCH),
                             wkTd.rearrange("(dc p) f -> p dc f", p=128), "GW")

                w1T_r = w1T.rearrange("(dc p) f -> p dc f", p=128)
                w2T_r = w2T.rearrange("(a p) d -> p a d", p=128)
                p1_steps = []
                for ci in range(32):
                    dc, fq = ci // 4, ci % 4
                    p1_steps.append((w1T_r[:, dc, fq * 1024:(fq + 1) * 1024], 1))
                for ci in range(32):
                    p1_steps.append((w2T_r[:, ci, :], 1))

                NQ1 = 64
                q1_dma_done = [0] * NQ1
                q1_red_done = [0] * NQ1
                st1 = {"issued": 0, "reduced": 0}

                def q1_issue():
                    i = st1["issued"]
                    if i >= NQ1:
                        return
                    if i >= 4:
                        assert q1_red_done[i - 4] > 0
                        waits = (("V", q1_red_done[i - 4]),)
                    else:
                        waits = ()
                    src, nsub = p1_steps[i]
                    dst = qin[i % 4][:, :]
                    if nsub > 1:
                        dst = dst.rearrange("p (a d) -> p a d", a=nsub)
                    q1_dma_done[i] = S.dma("sync", dst, src, f"QI{i % 4}", waits=waits)
                    st1["issued"] += 1

                def q1_reduce():
                    i = st1["reduced"]
                    if i >= st1["issued"] or i >= NQ1:
                        return
                    S.do("vector",
                         lambda e, i=i: e.tensor_reduce(
                             qred[:, i * 8:(i + 1) * 8],
                             qin[i % 4][:].rearrange("p (a x) -> p a x", a=8),
                             AX.X, ALU.add, apply_absolute_value=True),
                         waits=((f"QI{i % 4}", q1_dma_done[i]),), incs=(("V", 1),))
                    q1_red_done[i] = S.c["V"]
                    st1["reduced"] += 1

                for _ in range(4):
                    q1_issue()

                xb_dma_done = [0] * 8
                kps_evac = [0, 0]
                vps_evac = [0, 0]
                qps_evac = [0, 0]
                xb_dma_done[0] = emit_xb_dma(0, 0)
                gw_v = S.dma("gpsimd", wv_bf[:].rearrange("p (dc f) -> p dc f", dc=DCH),
                             wvT.rearrange("(dc p) f -> p dc f", p=128), "GW")
                xb_dma_done[1] = emit_xb_dma(1, 0)
                gw_q = S.dma("gpsimd", wq_bf[:].rearrange("p (dc f) -> p dc f", dc=DCH),
                             wqT.rearrange("(dc p) f -> p dc f", p=128), "GW")

                for cb in range(6):
                    x = xb[cb % 2]
                    first_wait = ((f"XB{cb % 2}", xb_dma_done[cb]), ("GW", gw_k))
                    for fc in range(4):
                        slot = fc % 2
                        w = (("A", kps_evac[slot]),) + (first_wait if fc < 2 else ())
                        for dc in range(DCH):
                            S.do("tensor",
                                 lambda e, dc=dc, fc=fc, x=x, slot=slot: e.matmul(
                                     kps[slot][:, :],
                                     wkd_bf[:, dc * 512 + fc * 128: dc * 512 + fc * 128 + 128],
                                     x[:, dc * 512:(dc + 1) * 512],
                                     start=(dc == 0), stop=(dc == DCH - 1)),
                                 waits=(w if dc == 0 else ()),
                                 incs=((("T", 1),) if dc == DCH - 1 else ()))
                        t_done = S.c["T"]
                        S.do("scalar",
                             lambda e, fc=fc, cb=cb, slot=slot: e.copy(
                                 KTd[:, fc * CTX + cb * 512: fc * CTX + cb * 512 + 512],
                                 kps[slot][:, :]),
                             waits=(("T", t_done),), incs=(("A", 1),))
                        kps_evac[slot] = S.c["A"]
                    for tc in range(4):
                        slot = tc % 2
                        w = (("A", vps_evac[slot]),) + ((("GW", gw_v),) if cb == 0 and tc == 0 else ())
                        for dc in range(DCH):
                            S.do("tensor",
                                 lambda e, dc=dc, tc=tc, x=x, slot=slot: e.matmul(
                                     vps[slot][:, 0:256],
                                     x[:, dc * 512 + tc * 128: dc * 512 + tc * 128 + 128],
                                     wv_bf[:, dc * 256:(dc + 1) * 256],
                                     start=(dc == 0), stop=(dc == DCH - 1)),
                                 waits=(w if dc == 0 else ()),
                                 incs=((("T", 1),) if dc == DCH - 1 else ()))
                        t_done = S.c["T"]
                        S.do("scalar",
                             lambda e, cb=cb, tc=tc, slot=slot: e.copy(
                                 Vv[:, (cb * 4 + tc) * 256:(cb * 4 + tc) * 256 + 256],
                                 vps[slot][:, 0:256]),
                             waits=(("T", t_done),), incs=(("A", 1),))
                        vps_evac[slot] = S.c["A"]
                        q1_reduce()
                        q1_issue()
                    if cb >= 2:
                        qcol = (cb - 2) * 512
                        for fc in range(DCH):
                            slot = fc % 2
                            w = (("A", qps_evac[slot]),) + ((("GW", gw_q),) if cb == 2 and fc == 0 else ())
                            for dc in range(DCH):
                                S.do("tensor",
                                     lambda e, dc=dc, fc=fc, x=x, slot=slot: e.matmul(
                                         qps[slot][:, :],
                                         wq_bf[:, dc * 1024 + fc * 128: dc * 1024 + fc * 128 + 128],
                                         x[:, dc * 512:(dc + 1) * 512],
                                         start=(dc == 0), stop=(dc == DCH - 1)),
                                     waits=(w if dc == 0 else ()),
                                     incs=((("T", 1),) if dc == DCH - 1 else ()))
                            t_done = S.c["T"]
                            S.do("scalar",
                                 lambda e, fc=fc, qcol=qcol, slot=slot: e.copy(
                                     QT[:, fc * QTOK + qcol: fc * QTOK + qcol + 512],
                                     qps[slot][:, :]),
                                 waits=(("T", t_done),), incs=(("A", 1),))
                            qps_evac[slot] = S.c["A"]
                            q1_reduce()
                            q1_issue()
                    if cb + 2 < 6:
                        xb_dma_done[cb + 2] = emit_xb_dma(cb + 2, S.c["T"])

                S.barrier(drain=("XB0", "XB1", "GW"))
                if stop_after == "P1":
                    _early_out(qred[:, :])
                S.flush()
            if stop_after == "P1":
                return nc

            with ExitStack() as p2:
                sb2 = lambda name, shape, dt: p2.enter_context(nc.sbuf_tensor(name, shape, dt))
                Tst = sb2("Tst", [128, 4608], F32)
                Pst = [sb2(f"Pst{i}", [128, 4608], BF16) for i in range(2)]
                den_sb = sb2("den_sb", [128, 512], F32)
                spsP = [p2.enter_context(nc.psum_tensor(f"spsP{i}", [128, 1024], F32))
                        for i in range(2)]
                pvd = p2.enter_context(nc.psum_tensor("pvd", [128, 512], F32))
                wopsb = [p2.enter_context(nc.psum_tensor(f"wops{i}", [128, 512],
                                                         F32)) for i in range(2)]
                lnst = p2.enter_context(nc.psum_tensor("lnst", [128, 512], F32))
                xr = [sb2(f"xr{i}", [128, 256], F32) for i in range(2)]
                r1f = [sb2(f"r1f{i}", [128, 256], F32) for i in range(2)]
                comb = [sb2(f"comb{i}", [128, 512], BF16) for i in range(4)]

                w1T_q = w1T.rearrange("(dc p) f -> p dc f", p=128)
                w2T_q = w2T.rearrange("(fq fs p) d -> p fq fs d", fq=4, p=128)
                NQ2 = 64
                q2 = [("w1", fc) for fc in range(NFC)] + [("w2", i) for i in range(32)]
                q2_dma_done = [0] * NQ2
                q2_cmp_done = [0] * NQ2
                q2_out_done = [0] * NQ2
                st2 = {"issued": 0, "done": 0}

                def q2_issue():
                    i = st2["issued"]
                    if i >= NQ2 or not q2_on:
                        return
                    kind, idx = q2[i]
                    if kind == "w1":
                        src = w1T_q[:, :, idx * 128:(idx + 1) * 128]
                        dst = qin[i % 4][:].rearrange("p (a f) -> p a f", a=DCH)
                    else:
                        fq, dc = idx // 8, idx % 8
                        src = w2T_q[:, fq, :, dc * 128:(dc + 1) * 128]
                        dst = qin[i % 4][:].rearrange("p (a f) -> p a f", a=8)
                    if i >= 4:
                        assert q2_cmp_done[i - 4] > 0
                        waits = (("V", q2_cmp_done[i - 4]),)
                    else:
                        assert v_th_box[0] > 0
                        waits = (("V", v_th_box[0]),)
                    q2_dma_done[i] = S.dma("sync", dst, src, f"QI{i % 4}", waits=waits)
                    st2["issued"] += 1

                def q2_step():
                    i = st2["done"]
                    if i >= st2["issued"] or i >= NQ2:
                        return
                    kind, idx = q2[i]
                    neg = nth1 if kind == "w1" else nth2
                    pos = th1 if kind == "w1" else th2
                    waits = [(f"QI{i % 4}", q2_dma_done[i])]
                    if i >= 2:
                        waits.append((f"QO{i % 2}", q2_out_done[i - 2]))
                    S.do("vector",
                         lambda e, i=i, neg=neg: e.tensor_single_scalar(
                             qx[:, :], qin[i % 4][:, :], neg[:, :], ALU.is_lt),
                         waits=tuple(waits), incs=(("V", 1),))
                    S.do("vector",
                         lambda e, i=i, pos=pos: e.scalar_tensor_tensor(
                             qout[i % 2][:, :], qin[i % 4][:, :], pos[:, :],
                             qx[:, :], ALU.is_gt, ALU.subtract),
                         incs=(("V", 1),))
                    q2_cmp_done[i] = S.c["V"]
                    if kind == "w1":
                        dst = w1q_t[idx]
                    else:
                        fq, dc = idx // 8, idx % 8
                        dst = w2q_t[dc][:, fq * 1024:(fq + 1) * 1024]
                    q2_out_done[i] = S.dma(
                        "gpsimd", dst, qout[i % 2][:, :], f"QO{i % 2}",
                        waits=(("V", q2_cmp_done[i]),))
                    st2["done"] += 1

                v_th_box = [0]

                def emit_gamma():
                    S.do("vector", lambda e: e.tensor_reduce(
                        gpart[:, 0:16],
                        qred[:, 0:256].rearrange("p (a x) -> p a x", a=16),
                        AX.X, ALU.add),
                        waits=(("V", q1_red_done[31]),), incs=(("V", 1),))
                    S.do("vector", lambda e: e.tensor_reduce(
                        gpart[:, 16:32],
                        qred[:, 256:512].rearrange("p (a x) -> p a x", a=16),
                        AX.X, ALU.add),
                        waits=(("V", q1_red_done[63]),), incs=(("V", 1),))
                    v_g = S.c["V"]
                    S.do("tensor", lambda e: e.matmul(
                        lnst[:, 0:16], ones_f32[:, :], gpart[:, 0:16],
                        start=True, stop=True),
                        waits=(("V", v_g),), incs=(("T", 1),))
                    S.do("tensor", lambda e: e.matmul(
                        lnst[:, 16:32], ones_f32[:, :], gpart[:, 16:32],
                        start=True, stop=True), incs=(("T", 1),))
                    t_g = S.c["T"]
                    S.do("vector", lambda e: e.tensor_reduce(
                        gsum[:, 0:1], lnst[:, 0:16], AX.X, ALU.add),
                        waits=(("T", t_g),), incs=(("V", 1),))
                    S.do("vector", lambda e: e.tensor_reduce(
                        gsum[:, 1:2], lnst[:, 16:32], AX.X, ALU.add),
                        incs=(("V", 1),))
                    S.do("vector", lambda e: e.tensor_scalar(
                        th1[:, :], gsum[:, 0:1], 0.5 / QCNT, 0.5 * EPS,
                        ALU.mult, ALU.add), incs=(("V", 1),))
                    S.do("vector", lambda e: e.tensor_scalar(
                        th2[:, :], gsum[:, 1:2], 0.5 / QCNT, 0.5 * EPS,
                        ALU.mult, ALU.add), incs=(("V", 1),))
                    S.do("vector", lambda e: e.tensor_single_scalar(
                        nth1[:, :], th1[:, :], -1.0, ALU.mult), incs=(("V", 1),))
                    S.do("vector", lambda e: e.tensor_single_scalar(
                        nth2[:, :], th2[:, :], -1.0, ALU.mult), incs=(("V", 1),))
                    v_th_box[0] = S.c["V"]

                def quant_pump(g):
                    if st1["reduced"] < NQ1:
                        for _ in range(2):
                            q1_reduce()
                            q1_issue()
                        if st1["reduced"] >= NQ1:
                            emit_gamma()
                        return
                    if v_th_box[0] == 0:
                        emit_gamma()
                    q2_step()
                    q2_issue()
                    if st2["done"] < g - 6:
                        q2_step()
                        q2_issue()

                Pst_r = [p[:].rearrange("p (rel hp par x) -> p par rel hp x",
                                        rel=NREL, hp=2, par=2) for p in Pst]
                OTn_rv = [o[:].rearrange("p (ch q4 x) -> p ch q4 x", ch=DCH, q4=4)
                          for o in OTn]

                piece_T = {}
                mask_V = {}
                tanh_done = {}
                exp_done = {}
                pv_done = {}
                norm_done = {}
                NG = NQT * 4

                def stg_off(kb, par):
                    if kb < 8:
                        return (kb // 2) * 1024 + par * 512 + (kb % 2) * 256
                    return 4096 + par * 256

                tanhP = {}

                def emit_scores_acts(g):
                    qt, kv = divmod(g, 4)
                    for kbp in range(5):
                        na = 2 if kbp < 4 else 1
                        edge = kbp == 0 or kbp == 4
                        ps = spsP[(5 * g + kbp) % 2]
                        for par in range(2):
                            base = par * 64
                            for a in range(na):
                                kb = 2 * kbp + a
                                ca = qt + kb
                                lhsT = KTd[base:base + 64,
                                           kv * CTX + ca * 128:
                                           kv * CTX + ca * 128 + 128]
                                for he in range(2):
                                    hg = kv * 4 + 2 * he + par
                                    col = par * 512 + a * 256 + he * 128
                                    rhs = QT[base:base + 64,
                                             (hg // 2) * QTOK + qt * 128:
                                             (hg // 2) * QTOK + qt * 128 + 128]
                                    waits = ()
                                    if par == 0 and a == 0 and he == 0:
                                        pp = 5 * g + kbp - 2
                                        prev = divmod(pp, 5) if pp >= 0 else None
                                        if prev in tanhP:
                                            waits = (("A", tanhP[prev]),)
                                    first_in_bank = (a == 0) and (he == 0)
                                    last_in_bank = (a == na - 1) and (he == 1) \
                                        and not edge
                                    piece_last = last_in_bank and par == 1
                                    S.do("tensor",
                                         lambda e, ps=ps, col=col, lhsT=lhsT,
                                         rhs=rhs, st=first_in_bank,
                                         sp=last_in_bank: e.matmul(
                                             ps[:, col:col + 128], lhsT, rhs,
                                             start=st, stop=sp),
                                         waits=waits,
                                         incs=((("T", 1),) if piece_last else ()))
                            if edge:
                                moff = 0 if kbp == 0 else 512
                                S.do("tensor",
                                     lambda e, ps=ps, par=par, moff=moff: e.matmul(
                                         ps[:, par * 512: par * 512 + 256],
                                         id_sb[:, :], tri_sb[:, moff:moff + 256],
                                         start=False, stop=True),
                                     incs=((("T", 1),) if par == 1 else ()))
                        t_piece = S.c["T"]
                        w = [("T", t_piece)]
                        if kbp < 4:
                            S.do("scalar",
                                 lambda e, ps=ps, kbp=kbp: e.activation(
                                     Tst[:, kbp * 1024:(kbp + 1) * 1024],
                                     ps[:, 0:1024], AF.Tanh, scale=1.0 / 240.0),
                                 waits=tuple(w), incs=(("A", 1),))
                        else:
                            S.do("scalar",
                                 lambda e, ps=ps: e.activation(
                                     Tst[:, 4096:4608],
                                     ps[:].rearrange("p (par x) -> p par x",
                                                     par=2)[:, :, 0:256],
                                     AF.Tanh, scale=1.0 / 240.0),
                                 waits=tuple(w), incs=(("A", 1),))
                        tanhP[(g, kbp)] = S.c["A"]
                    w = [("A", tanhP[(g, 4)])]
                    if g >= 2 and pv_done.get(g - 2, 0) > 0:
                        w.append(("T", pv_done[g - 2]))
                    if g == 0:
                        w.append(("V", v_consts), )
                    S.do("scalar",
                         lambda e, g=g: e.activation(
                             Pst[g % 2][:, :], Tst[:, :], AF.Exp, bias=cm30[:, :],
                             scale=30.0),
                         waits=tuple(w), incs=(("A", 1),))
                    exp_done[g] = S.c["A"]

                def emit_pv(g):
                    qt, kv = divmod(g, 4)
                    P = Pst[g % 2]
                    pd = pvd
                    for rel in range(NREL):
                        ca = qt + rel
                        vt = Vv[:, ca * 256 + kv * 64: ca * 256 + kv * 64 + 64]
                        for par in range(2):
                            w0 = []
                            if par == 0 and rel == 0:
                                w0.append(("A", exp_done[g]))
                                if g >= 1 and norm_done.get(g - 1, 0) > 0:
                                    w0.append(("V", norm_done[g - 1]))
                            tp = None if par == 0 else (0, 64)
                            out = pd[0:64, 0:256] if par == 0 else pd[64:128, 0:256]
                            rhs = P[:, stg_off(rel, par): stg_off(rel, par) + 256]
                            S.do("tensor",
                                 lambda e, out=out, vt=vt, rhs=rhs, rel=rel,
                                 tp=tp: e.matmul(
                                     out, vt, rhs, start=(rel == 0),
                                     stop=(rel == NREL - 1), tile_position=tp),
                                 waits=tuple(w0))
                    for rel in range(NREL):
                        ca = qt + rel
                        for par in range(2):
                            rhs = P[:, stg_off(rel, par): stg_off(rel, par) + 256]
                            w0 = []
                            if par == 0 and rel == 0 and g == 0:
                                w0.append(("LDS", ld_small))
                            tp = None if par == 0 else (0, 64)
                            out = (pd[0:64, 256:512] if par == 0
                                   else pd[64:128, 256:512])
                            S.do("tensor",
                                 lambda e, ca=ca, rhs=rhs, rel=rel, out=out,
                                 tp=tp: e.matmul(
                                     out, vrep_sb[:, ca * 128: ca * 128 + 64],
                                     rhs, start=(rel == 0),
                                     stop=(rel == NREL - 1), tile_position=tp),
                                 waits=tuple(w0),
                                 incs=((("T", 1),)
                                       if rel == NREL - 1 and par == 1 else ()))
                    pv_done[g] = S.c["T"]

                def emit_norm(g):
                    qt, kv = divmod(g, 4)
                    pd = pvd
                    w0 = [("T", pv_done[g])]
                    if g % 16 == 0 and (g // 16 - 2) in win_last_T:
                        w0.append(("T", win_last_T[g // 16 - 2]))
                    S.do("vector",
                         lambda e, pd=pd: e.reciprocal(den_sb[:, 0:256],
                                                       pd[:, 256:512]),
                         waits=tuple(w0), incs=(("V", 1),))
                    vr = S.c["V"]
                    slot = (qt // 4) % 2
                    for par in range(2):
                        S.do("vector",
                             lambda e, par=par, kv=kv, qt=qt, pd=pd,
                             slot=slot: e.tensor_tensor(
                                 OTn_rv[slot][par * 64:par * 64 + 64,
                                              2 * kv:2 * kv + 2, qt % 4],
                                 pd[par * 64:par * 64 + 64, 0:256],
                                 den_sb[par * 64:par * 64 + 64, 0:256],
                                 ALU.mult),
                             waits=((("V", vr),) if par == 0 else ()),
                             incs=((("V", 1),) if par == 1 else ()))
                    norm_done[g] = S.c["V"]

                fold_on = p2_ng is None
                woT_r = woT.rearrange("(dc p) f -> p dc f", p=128)
                wo_bf_r = wo_bf[:].rearrange("p (dc f) -> p dc f", dc=DCH)
                r1d_r = r1d.rearrange("(dc p) q -> p dc q", p=128)
                wo_gd = [0]
                evac_V = [0] * 64
                stats_T = [0] * 64
                t_mm_arr = [0] * 64
                r1s_cnt = [0] * 64
                xr_cnt = [0] * 64
                xr_issued = [False] * 64
                last_extr_V = [0]
                win_last_T = {}
                w1_pre = [0, 0]
                mrs_after = {}

                def xr_buf(u):
                    s = u % 4
                    if s < 2:
                        return xr[s][:, :]
                    return qred[:, (s - 2) * 256:(s - 1) * 256]

                def ensure_xr(u):
                    if u >= 64 or xr_issued[u]:
                        return
                    k, r = u // 16, u % 16
                    half, hc = r // 8, r % 8
                    c0 = k * 512 + half * 256
                    w = []
                    if u >= 4:
                        assert evac_V[u - 4] > 0
                        w.append(("V", evac_V[u - 4]))
                    elif u in (2, 3):
                        w.append(("V", v_th_box[0]))
                    xr_cnt[u] = S.dma(
                        "sync", xr_buf(u),
                        xctxT_r[:, hc, WINDOW + c0: WINDOW + c0 + 256],
                        f"XR{u % 4}", waits=tuple(w))
                    xr_issued[u] = True

                v_comb_arr = [0] * 64

                def wo_stageA(u):
                    k, r = u // 16, u % 16
                    half, hc = r // 8, r % 8
                    wop = wopsb[u % 2]
                    for jc in range(DCH):
                        w = []
                        if jc == 0:
                            w.append(("V", norm_done[k * 16 + 15]))
                            w.append(("GW", wo_gd[0]))
                            if u >= 2:
                                w.append(("V", evac_V[u - 2]))
                        S.do("tensor",
                             lambda e, jc=jc, hc=hc, k=k, half=half,
                             wop=wop: e.matmul(
                                 wop[:, 0:256],
                                 wo_bf[:, jc * 1024 + hc * 128:
                                       jc * 1024 + hc * 128 + 128],
                                 OTn[k % 2][:, jc * 512 + half * 256:
                                            jc * 512 + half * 256 + 256],
                                 start=(jc == 0), stop=(jc == DCH - 1)),
                             waits=tuple(w) if jc == 0 else (),
                             incs=((("T", 1),) if jc == DCH - 1 else ()))
                    stats_T[u] = 0
                    win_last_T[k] = S.c["T"]
                    t_mm_arr[u] = S.c["T"]

                def wo_stageB(u):
                    k, r = u // 16, u % 16
                    half, hc = r // 8, r % 8
                    c0 = k * 512 + half * 256
                    w = [("T", t_mm_arr[u]), (f"XR{u % 4}", xr_cnt[u])]
                    if u >= 2:
                        w.append((f"R1S{u % 2}", r1s_cnt[u - 2]))
                    S.do("vector", lambda e, u=u: e.tensor_tensor(
                        r1f[u % 2][:, :], wopsb[u % 2][:, 0:256],
                        xr_buf(u), ALU.add),
                        waits=tuple(w), incs=(("V", 1),))
                    evac_V[u] = S.c["V"]
                    r1s_cnt[u] = S.dma(
                        "gpsimd", r1d_r[:, hc, c0:c0 + 256], r1f[u % 2][:, :],
                        f"R1S{u % 2}", waits=(("V", evac_V[u]),))
                    w = []
                    if u >= 4:
                        w.append(("T", stats_T[u - 4]))
                    S.do("vector", lambda e, u=u: e.tensor_copy(
                        comb[u % 4][:, 0:256], r1f[u % 2][:, :]),
                        waits=tuple(w), incs=(("V", 1),))
                    S.do("vector", lambda e, u=u: e.tensor_tensor(
                        comb[u % 4][:, 256:512], r1f[u % 2][:, :],
                        r1f[u % 2][:, :], ALU.mult), incs=(("V", 1),))
                    v_comb_arr[u] = S.c["V"]

                def wo_stageC(u):
                    hc = u % 8
                    w = [("V", v_comb_arr[u])]
                    if hc == 0 and last_extr_V[0] > 0:
                        w.append(("V", last_extr_V[0]))
                    S.do("tensor", lambda e, u=u, hc=hc: e.matmul(
                        lnst[:, :], ones_bf[:, :], comb[u % 4][:, :],
                        start=(hc == 0), stop=(hc == DCH - 1)),
                        waits=tuple(w), incs=(("T", 1),))
                    stats_T[u] = S.c["T"]

                def wo_stageD(u):
                    k, r = u // 16, u % 16
                    half = r // 8
                    c0 = k * 512 + half * 256
                    S.do("vector", lambda e, c0=c0: e.tensor_single_scalar(
                        mean_all[:, c0:c0 + 256], lnst[:, 0:256],
                        1.0 / DM, ALU.mult),
                        waits=(("T", stats_T[u]),), incs=(("V", 1),))
                    S.do("vector", lambda e, c0=c0: e.tensor_tensor(
                        den_sb[:, 256:512], mean_all[:, c0:c0 + 256],
                        mean_all[:, c0:c0 + 256], ALU.mult), incs=(("V", 1),))
                    S.do("vector", lambda e, c0=c0: e.scalar_tensor_tensor(
                        rstd_all[:, c0:c0 + 256], lnst[:, 256:512], 1.0 / DM,
                        den_sb[:, 256:512], ALU.mult, ALU.subtract),
                        incs=(("V", 1),))
                    last_extr_V[0] = S.c["V"]
                    S.dma("gpsimd", meand[:, c0:c0 + 256],
                          mean_all[:, c0:c0 + 256], "MRS",
                          waits=(("V", last_extr_V[0]),))
                    S.dma("gpsimd", rstdd[:, c0:c0 + 256],
                          rstd_all[:, c0:c0 + 256], "MRS",
                          waits=(("V", last_extr_V[0]),))
                    mrs_after[(k, half)] = S.c["MRS"]

                def wo_pump(i):
                    uD = i - 20
                    if 0 <= uD < 64 and uD % 8 == 7:
                        wo_stageD(uD)
                    uC = i - 19
                    if 0 <= uC < 64:
                        wo_stageC(uC)
                    uB = i - 18
                    if 0 <= uB < 64:
                        wo_stageB(uB)

                def wo_pump_a(i):
                    uA = i - 17
                    if 0 <= uA < 64:
                        ensure_xr(uA)
                        ensure_xr(uA + 1)
                        ensure_xr(uA + 2)
                        ensure_xr(uA + 3)
                        wo_stageA(uA)

                for g in range(NG if p2_ng is None else p2_ng):
                    if fold_on and g < DCH:
                        wo_gd[0] = S.dma("gpsimd", wo_bf_r[:, g, :],
                                         woT_r[:, g, :], "GW")
                    quant_pump(g)
                    if fold_on:
                        wo_pump(g)
                    emit_scores_acts(g)
                    if g >= 1:
                        emit_pv(g - 1)
                        emit_norm(g - 1)
                    if fold_on:
                        wo_pump_a(g)
                _ng = NG if p2_ng is None else p2_ng
                emit_pv(_ng - 1)
                emit_norm(_ng - 1)
                if fold_on:
                    for i in range(_ng, _ng + 22):
                        wo_pump(i)
                        wo_pump_a(i)
                while st1["reduced"] < NQ1:
                    q1_reduce()
                    q1_issue()
                if v_th_box[0] == 0:
                    emit_gamma()
                while q2_on and st2["done"] < NQ2:
                    q2_step()
                    q2_issue()
                qo_final = (S.c["QO0"], S.c["QO1"])

        otn_ctx.close()
        x2 = sb("x2", [128, DCH * QTOK], F32)
        mean4 = sb("mean4", [128, 1024], F32)
        rstd4 = sb("rstd4", [128, 1024], F32)
        f_tmp = sb("f_tmp", [128, 512], F32)
        sig_sb = sb("sig_sb4", [128, 512], F32)
        x2bf = sb("x2bf", [128, DCH * 1024], BF16)
        x2_r = x2[:].rearrange("p (dc c) -> p dc c", dc=DCH)
        BDRAIN = ("QI0", "QI1", "QI2", "QI3", "QO0", "QO1", "LDS", "GW",
                  "XR0", "XR1", "XR2", "XR3", "MRS", "R1S0", "R1S1")

        if stop_after == "P2":
            S.barrier(drain=BDRAIN)
            _early_out(Tst[:, 0:512] if p2_level < 4 else den_sb[:, :])
            S.flush()
            return nc

        pvT = pv_done[_ng - 1]
        S.dma("sync", mean4[:, :], meand[:, 0:1024], "MRL",
              waits=(("T", pvT), ("MRS", mrs_after[(1, 1)])))
        mrl_cnt = [0, 0]
        mrl_cnt[0] = S.dma("sync", rstd4[:, :], rstdd[:, 0:1024], "MRL")
        rl_tot = {}
        for t in (0, 1):
            for dc in range(DCH):
                eng, q = (("sync", "RL") if (t == 0 and dc % 2 == 0)
                          else ("gpsimd", "RLG"))
                w = ()
                if dc == 0 or (t == 0 and dc == 1):
                    w = (("T", pvT),
                         ("R1S0", r1s_cnt[30 if t == 0 else 62]),
                         ("R1S1", r1s_cnt[31 if t == 0 else 63]))
                S.dma(eng, x2_r[:, dc, t * 1024:(t + 1) * 1024],
                      r1d_r[:, dc, t * 1024:(t + 1) * 1024], q, waits=w)
            rl_tot[t] = (S.c["RL"], S.c["RLG"])

        g_norm = [0, 0]

        def emit_norm_cast(t, wait_t):
            for ch in range(2):
                w = []
                if ch == 0:
                    w = [("MRL", mrl_cnt[t])]
                    if t == 0:
                        w += [("T", pvT), ("V", q2_cmp_done[NQ2 - 1])]
                S.do("vector", lambda e, ch=ch: e.tensor_single_scalar(
                    f_tmp[:, :], rstd4[:, ch * 512:(ch + 1) * 512],
                    EPS, ALU.add), waits=tuple(w), incs=(("V", 1),))
                v_a = S.c["V"]
                S.do("scalar", lambda e: e.activation(
                    sig_sb[:, :], f_tmp[:, :], AF.Sqrt),
                    waits=(("V", v_a),), incs=(("A", 1),))
                a_s = S.c["A"]
                S.do("vector", lambda e, ch=ch: e.reciprocal(
                    rstd4[:, ch * 512:(ch + 1) * 512], sig_sb[:, :]),
                    waits=(("A", a_s),), incs=(("V", 1),))
            v_conv = S.c["V"]
            eng = "vector" if t == 0 else "gpsimd"
            esem = "V" if t == 0 else "G"
            tmp = f_tmp if t == 0 else tmp_g
            first = True
            for dc in range(DCH):
                for h2 in range(2):
                    c0 = dc * QTOK + t * 1024 + h2 * 512
                    mc = h2 * 512
                    w = []
                    if first:
                        w = [("RL", rl_tot[t][0]), ("RLG", rl_tot[t][1]),
                             ("MRL", mrl_cnt[t])]
                        if eng == "gpsimd":
                            w.append(("V", v_conv))
                        first = False
                    S.do(eng, lambda e, c0=c0, mc=mc, tmp=tmp: e.tensor_tensor(
                        tmp[:, :], x2[:, c0:c0 + 512], mean4[:, mc:mc + 512],
                        ALU.subtract), waits=tuple(w), incs=((esem, 1),))
                    S.do(eng, lambda e, c0=c0, mc=mc, tmp=tmp: e.tensor_tensor(
                        x2[:, c0:c0 + 512], tmp[:, :], rstd4[:, mc:mc + 512],
                        ALU.mult), incs=((esem, 1),))
                ecnt = S.c[esem]
                wc = []
                if eng == "gpsimd":
                    wc.append(("G", ecnt))
                if dc == 0 and wait_t > 0:
                    wc.append(("T", wait_t))
                S.do("vector", lambda e, dc=dc, t=t: e.tensor_copy(
                    x2bf[:, dc * 1024:(dc + 1) * 1024],
                    x2[:, dc * QTOK + t * 1024: dc * QTOK + t * 1024 + 1024]),
                    waits=tuple(wc), incs=(("V", 1),))
            g_norm[t] = S.c[esem]
            return S.c["V"]

        cast_ready = emit_norm_cast(0, 0)
        for fc in range(2):
            w1_pre[fc] = S.dma(
                "sync", w1b[fc][:, :], w1q_t[fc], f"W1{fc}",
                waits=(("QO0", qo_final[0]), ("QO1", qo_final[1])))

        S.barrier(drain=BDRAIN)
        S.flush()

        with ExitStack() as p4:
            sb4 = lambda name, shape, dt: p4.enter_context(nc.sbuf_tensor(name, shape, dt))
            hT = sb4("hT", [128, NFC * 1024], BF16)
            w2b = [sb4(f"w2b{i}", [128, DFF], BF16) for i in range(3)]
            t_bf = sb4("t_bf4", [128, 512], BF16)
            t_sq = sb4("t_sq4", [128, 512], BF16)
            mean_sb = sb4("mean_sb4", [128, 512], F32)
            tmp_a = sb4("tmp_a4", [128, 512], F32)
            tmp_b = sb4("tmp_b4", [128, 512], F32)
            rstd_sb = sb4("rstd_sb4", [128, 512], F32)
            tmp_g = sb4("tmp_g4", [128, 512], F32)
            f1ps = [p4.enter_context(nc.psum_tensor(f"f1ps{i}", [128, 512], F32))
                    for i in range(2)]
            f2ps = [p4.enter_context(nc.psum_tensor(f"f2ps{i}", [128, 512], F32))
                    for i in range(2)]
            sum_ps = [p4.enter_context(nc.psum_tensor(f"sum_ps4{i}", [128, 512], F32))
                      for i in range(2)]
            sq_ps = [p4.enter_context(nc.psum_tensor(f"sq_ps4{i}", [128, 512], F32))
                     for i in range(2)]

            w1_dma = [0] * NFC
            w2_dma = [0] * (2 * DCH)
            w1_use = [0, 0, 0]
            w2_use = [0, 0, 0]
            relu_use = [0, 0]
            f2_evac = [0, 0]
            stats_guard4 = [0, 0]
            prev_stats4 = [0]

            g_guard4 = [0]

            def ln_tail4(slot, col0, stats_wait):
                sm, sq = sum_ps[slot], sq_ps[slot]
                w0 = tuple(stats_wait)
                if g_guard4[0] > 0:
                    w0 = w0 + (("G", g_guard4[0]),)
                S.do("vector", lambda e: e.tensor_single_scalar(
                    mean_sb[:, :], sm[:, :], 1.0 / DM, ALU.mult),
                    waits=w0, incs=(("V", 1),))
                S.do("vector", lambda e: e.tensor_tensor(
                    tmp_a[:, :], mean_sb[:, :], mean_sb[:, :], ALU.mult),
                    incs=(("V", 1),))
                S.do("vector", lambda e: e.scalar_tensor_tensor(
                    tmp_b[:, :], sq[:, :], 1.0 / DM, tmp_a[:, :],
                    ALU.mult, ALU.subtract), incs=(("V", 1),))
                stats_guard4[slot] = S.c["V"]
                S.do("vector", lambda e: e.tensor_single_scalar(
                    tmp_b[:, :], tmp_b[:, :], EPS, ALU.add), incs=(("V", 1),))
                v_var = S.c["V"]
                S.do("scalar", lambda e: e.activation(sig_sb[:, :], tmp_b[:, :], AF.Sqrt),
                     waits=(("V", v_var),), incs=(("A", 1),))
                a_sig = S.c["A"]
                S.do("vector", lambda e: e.reciprocal(rstd_sb[:, :], sig_sb[:, :]),
                     waits=(("A", a_sig),), incs=(("V", 1),))
                v_rstd = S.c["V"]
                for dc in range(DCH):
                    c0 = dc * QTOK + col0
                    S.do("gpsimd", lambda e, c0=c0: e.tensor_tensor(
                        tmp_g[:, :], x2[:, c0:c0 + 512], mean_sb[:, :], ALU.subtract),
                        waits=((("V", v_rstd),) if dc == 0 else ()), incs=(("G", 1),))
                    S.do("gpsimd", lambda e, c0=c0: e.tensor_tensor(
                        x2[:, c0:c0 + 512], tmp_g[:, :], rstd_sb[:, :], ALU.mult),
                        incs=(("G", 1),))
                g_guard4[0] = S.c["G"]
                return S.c["G"]

            outT_r = outT.rearrange("(dc p) q -> p dc q", p=128)

            def issue_w1(fc, extra=()):
                wslot = fc % 3
                w = list(extra)
                if w1_use[wslot] > 0:
                    w.append(("T", w1_use[wslot]))
                w1_dma[fc] = S.dma("sync", w1b[wslot][:, :], w1q_t[fc],
                                   f"W1{wslot}", waits=tuple(w))

            def issue_w2(i2, dc, extra=()):
                wslot = i2 % 3
                w = list(extra)
                if w2_use[wslot] > 0:
                    w.append(("T", w2_use[wslot]))
                w2_dma[i2] = S.dma("sync", w2b[wslot][:, :], w2q_t[dc],
                                   f"W2{wslot}", waits=tuple(w))

            ffn1_end = [0, 0]
            pending_ln = []

            def flush_ln():
                tcol, sdone = pending_ln.pop(0)
                for sg in range(2):
                    v_ln = ln_tail4(sg, tcol + sg * 512, (("T", sdone[sg]),))
                    c0 = tcol + sg * 512
                    S.do("gpsimd", lambda e, c0=c0: e.dma_start(
                        outT_r[:, :, c0:c0 + 512], x2_r[:, :, c0:c0 + 512]),
                        waits=(("G", v_ln), ("OST", S.c["OST"])) if S.c["OST"] > 0
                        else (("G", v_ln),),
                        incs=(("OST", 16),))

            for t in range(2):
                tcol = t * 1024
                x2bf_ready = cast_ready

                if t == 0:
                    w1_dma[0], w1_dma[1] = w1_pre
                    issue_w1(2)
                else:
                    for fc0 in range(3):
                        issue_w1(fc0)
                for fc in range(NFC):
                    wslot = fc % 3
                    for nt in range(2):
                        pslot = (fc * 2 + nt) % 2
                        for dc in range(DCH):
                            w = []
                            if dc == 0:
                                w = [(f"W1{fc % 3}", w1_dma[fc]), ("V", x2bf_ready)]
                                if relu_use[pslot] > 0:
                                    w.append(("A", relu_use[pslot]))
                            S.do("tensor",
                                 lambda e, dc=dc, nt=nt, wslot=wslot, pslot=pslot: e.matmul(
                                     f1ps[pslot][:, :],
                                     w1b[wslot][:, dc * 128:(dc + 1) * 128],
                                     x2bf[:, dc * 1024 + nt * 512: dc * 1024 + nt * 512 + 512],
                                     start=(dc == 0), stop=(dc == DCH - 1)),
                                 waits=tuple(w) if dc == 0 else (),
                                 incs=((("T", 1),) if dc == DCH - 1 else ()))
                        t_f1 = S.c["T"]
                        if nt == 1:
                            w1_use[wslot] = t_f1
                            if fc + 3 < NFC:
                                issue_w1(fc + 3)
                        S.do("scalar",
                             lambda e, fc=fc, nt=nt, pslot=pslot: e.activation(
                                 hT[:, fc * 1024 + nt * 512: fc * 1024 + nt * 512 + 512],
                                 f1ps[pslot][:, :], AF.Relu,
                                 bias=b1_sb[:, fc:fc + 1], scale=0.5),
                             waits=(("T", t_f1),), incs=(("A", 1),))
                        relu_use[pslot] = S.c["A"]
                a_relu_all = S.c["A"]
                ffn1_end[t] = S.c["T"]
                if t == 0:
                    S.dma("sync", mean4[:, :], meand[:, 1024:2048], "MRL",
                          waits=(("V", g_norm[0]),))
                    mrl_cnt[1] = S.dma("sync", rstd4[:, :], rstdd[:, 1024:2048],
                                       "MRL")
                    cast_ready = emit_norm_cast(1, ffn1_end[0])
                if t == 1:
                    flush_ln()

                stats_done4 = [0, 0]
                for nt in range(2):
                    if nt == 0:
                        for j in range(3):
                            issue_w2(j, j)
                    for dc in range(DCH):
                        gi = nt * DCH + dc
                        wslot = gi % 3
                        pslot = gi % 2
                        for fc in range(NFC):
                            w = []
                            if fc == 0:
                                w = [(f"W2{gi % 3}", w2_dma[gi]), ("A", a_relu_all)]
                                if f2_evac[pslot] > 0:
                                    w.append(("V", f2_evac[pslot]))
                            S.do("tensor",
                                 lambda e, fc=fc, nt=nt, wslot=wslot, pslot=pslot: e.matmul(
                                     f2ps[pslot][:, :],
                                     w2b[wslot][:, fc * 128:(fc + 1) * 128],
                                     hT[:, fc * 1024 + nt * 512: fc * 1024 + nt * 512 + 512],
                                     start=(fc == 0), stop=(fc == NFC - 1)),
                                 waits=tuple(w) if fc == 0 else (),
                                 incs=((("T", 1),) if fc == NFC - 1 else ()))
                        t_f2 = S.c["T"]
                        w2_use[wslot] = t_f2
                        if gi + 3 < 2 * DCH:
                            issue_w2(gi + 3, (gi + 3) % DCH)
                        wv = [("T", t_f2)]
                        if t == 1 and nt == 0 and dc == 0 and g_norm[1] > 0:
                            wv.append(("G", g_norm[1]))
                        S.do("vector",
                             lambda e, dc=dc, pslot=pslot: e.tensor_scalar(
                                 f_tmp[:, :], f2ps[pslot][:, :], 0.5,
                                 b2_sb[:, dc:dc + 1], ALU.mult, ALU.add),
                             waits=tuple(wv), incs=(("V", 1),))
                        c0 = dc * QTOK + tcol + nt * 512
                        S.do("vector", lambda e, c0=c0: e.tensor_tensor(
                            x2[:, c0:c0 + 512], f_tmp[:, :], x2[:, c0:c0 + 512], ALU.add),
                            incs=(("V", 1),))
                        f2_evac[pslot] = S.c["V"]
                        w_cast4 = [("V", f2_evac[pslot])]
                        if prev_stats4[0] > 0:
                            w_cast4.append(("T", prev_stats4[0]))
                        S.do("scalar", lambda e, c0=c0: e.copy(
                            t_bf[:, :], x2[:, c0:c0 + 512]),
                            waits=tuple(w_cast4), incs=(("A", 1),))
                        S.do("scalar", lambda e, c0=c0: e.activation(
                            t_sq[:, :], x2[:, c0:c0 + 512], AF.Square),
                            incs=(("A", 1),))
                        v_tiles = S.c["A"]
                        w_sum = [("A", v_tiles)]
                        if dc == 0 and stats_guard4[nt] > 0:
                            w_sum.append(("V", stats_guard4[nt]))
                        S.do("tensor", lambda e, dc=dc, nt=nt: e.matmul(
                            sum_ps[nt][:, :], ones_bf[:, :], t_bf[:, :],
                            start=(dc == 0), stop=(dc == DCH - 1)),
                            waits=tuple(w_sum))
                        S.do("tensor", lambda e, dc=dc, nt=nt: e.matmul(
                            sq_ps[nt][:, :], ones_bf[:, :], t_sq[:, :],
                            start=(dc == 0), stop=(dc == DCH - 1)),
                            incs=(("T", 1),))
                        prev_stats4[0] = S.c["T"]
                        stats_done4[nt] = S.c["T"]
                    if t == 1 and nt == 0:
                        v_ln = ln_tail4(0, tcol, (("T", stats_done4[0]),))
                        S.do("gpsimd", lambda e: e.dma_start(
                            outT_r[:, :, tcol:tcol + 512],
                            x2_r[:, :, tcol:tcol + 512]),
                            waits=(("G", v_ln), ("OST", S.c["OST"])) if S.c["OST"] > 0
                            else (("G", v_ln),),
                            incs=(("OST", 16),))

                if t == 0:
                    pending_ln.append((tcol, tuple(stats_done4)))
                else:
                    v_ln = ln_tail4(1, tcol + 512, (("T", stats_done4[1]),))
                    S.do("gpsimd", lambda e: e.dma_start(
                        outT_r[:, :, tcol + 512:tcol + 1024],
                        x2_r[:, :, tcol + 512:tcol + 1024]),
                        waits=(("G", v_ln), ("OST", S.c["OST"])),
                        incs=(("OST", 16),))

            S.do("gpsimd", None, waits=(("OST", S.c["OST"]),))
            S.flush()

    return nc


_NC = None


def _get_nc():
    global _NC
    if _NC is None:
        _NC = build_nc()
    return _NC


def _host_inputs(inputs):
    x = np.asarray(inputs["L0_attn_x"], dtype=np.float32)
    wq = np.asarray(inputs["L0_attn_Wq"], dtype=np.float32)
    wk = np.asarray(inputs["L0_attn_Wk"], dtype=np.float32)
    wv = np.asarray(inputs["L0_attn_Wv"], dtype=np.float32)
    wo = np.asarray(inputs["L0_attn_Wo"], dtype=np.float32)
    w1 = np.asarray(inputs["L0_ffn_W1"], dtype=np.float32)
    b1 = np.asarray(inputs["L0_ffn_b1"], dtype=np.float32)
    w2 = np.asarray(inputs["L0_ffn_W2"], dtype=np.float32)
    b2 = np.asarray(inputs["L0_ffn_b2"], dtype=np.float32)

    wqT = np.ascontiguousarray(wq.T)
    wkT = wk.T
    wkTd = np.ascontiguousarray(
        np.repeat(wkT.reshape(DM, NKV, HD), 2, axis=1).reshape(DM, 512))
    wvT = np.ascontiguousarray(wv.T)
    woT = np.ascontiguousarray(wo.T)
    w1T = np.ascontiguousarray(w1.T)
    w2T = np.ascontiguousarray(w2.T)
    b1c = np.ascontiguousarray(b1.reshape(NFC, 128).T)
    b2c = np.ascontiguousarray(b2.reshape(DCH, 128).T)

    u = np.arange(128)[:, None]
    qi = np.arange(128)[None, :]
    m0 = np.where(u <= qi, NEG, 0.0).astype(np.float32)
    m8 = np.where(u > qi, NEG, 0.0).astype(np.float32)
    trim = np.ascontiguousarray(
        np.concatenate([np.tile(m0, (1, 4)), np.tile(m8, (1, 4))],
                       axis=1)).astype(ml_dtypes.bfloat16)
    ident = np.eye(128, dtype=ml_dtypes.bfloat16)

    shared = dict(wqT=wqT, wkTd=wkTd, wvT=wvT, woT=woT, w1T=w1T, w2T=w2T,
                  b1c=b1c, b2c=b2c, trimask=trim, identin=ident)

    in_maps = []
    for c in range(8):
        b, h = c // 2, c % 2
        qstart = h * QTOK
        xc = np.zeros((CTX, DM), dtype=np.float32)
        lo = qstart - WINDOW
        if lo < 0:
            xc[-lo:, :] = x[b, 0:qstart + QTOK, :]
        else:
            xc[:, :] = x[b, lo:qstart + QTOK, :]
        xctxT = np.ascontiguousarray(xc.T)
        valid = np.ones((CTX, 128), dtype=ml_dtypes.bfloat16)
        if lo < 0:
            valid[0:-lo, :] = 0
        m = dict(shared)
        m["xctxT"] = xctxT
        m["validrep"] = valid
        in_maps.append(m)
    return in_maps


def kernel(**inputs):
    nc = _get_nc()
    in_maps = _host_inputs(inputs)
    res = run_bass_kernel_spmd(nc, in_maps, core_ids=list(range(8)))
    out = np.empty((B, S, DM), dtype=np.float32)
    for c in range(8):
        b, h = c // 2, c % 2
        out[b, h * QTOK:(h + 1) * QTOK, :] = res.results[c]["outT"].T
    return out


# version 26
# speedup vs baseline: 1.0433x; 1.0433x over previous
import numpy as np
import ml_dtypes
from contextlib import ExitStack

import concourse.bass as bass
import concourse.mybir as mybir
from concourse.bass_utils import run_bass_kernel_spmd

F32 = mybir.dt.float32
BF16 = mybir.dt.bfloat16
AF = mybir.ActivationFunctionType
ALU = mybir.AluOpType
AX = mybir.AxisListType

NEG = -1.0e30
B, S, DM = 4, 4096, 1024
NH, NKV, HD = 16, 4, 64
WINDOW = 1024
QTOK = 2048
CTX = 3072
DCH = DM // 128
NQT = QTOK // 128
NREL = 9
CTXCH = CTX // 128
DFF = 4096
NFC = DFF // 128
EPS = 1e-5
QCNT = float(DM * DFF)

ENGS = ("tensor", "vector", "scalar", "sync", "gpsimd")


class Sched:

    SEM_NAMES = ("T", "V", "A", "G", "BAR", "LDS", "GW", "XB0", "XB1",
                 "QI0", "QI1", "QI2", "QI3", "QO0", "QO1", "W10", "W11", "W12",
                 "W20", "W21", "W22", "OST", "XR0", "XR1", "XR2", "XR3", "R1S0", "R1S1",
                 "RL", "RLG", "MRL", "MRS")

    def __init__(self, nc, es):
        self.nc = nc
        self.sems = {}
        self.c = {}
        self.events = []
        for name in self.SEM_NAMES:
            self.sems[name] = es.enter_context(nc.semaphore(name))
            self.c[name] = 0

    AUTO_SELF = {"vector": "V", "scalar": "A", "gpsimd": "G"}

    def do(self, eng, fn=None, waits=(), incs=()):
        sem = self.AUTO_SELF.get(eng)
        if sem is not None:
            if self.c[sem] > 0:
                waits = ((sem, self.c[sem]),) + tuple(waits)
            if not any(s == sem for s, _ in incs):
                incs = tuple(incs) + ((sem, 1),)
        for s, v in waits:
            assert v <= self.c[s], f"wait {s}>={v} but only {self.c[s]} emitted"
        self.events.append((eng, tuple(waits), fn, tuple(incs)))
        for s, n in incs:
            self.c[s] += n

    def dma(self, qeng, dst, src, sem, waits=()):
        w = tuple(waits)
        if self.c[sem] > 0:
            w = ((sem, self.c[sem]),) + w
        self.do(qeng, lambda e, d=dst, s=src: e.dma_start(d, s), waits=w,
                incs=((sem, 16),))
        return self.c[sem]

    def barrier(self, drain=()):
        for e in ENGS:
            self.do(e, None, incs=(("BAR", 1),))
        tgt = self.c["BAR"]
        dw = tuple((s, self.c[s]) for s in drain)
        for e in ENGS:
            self.do(e, None, waits=(("BAR", tgt),) + dw)

    def flush(self):
        evs, self.events = self.events, []
        if not evs:
            return
        with self.nc.Block() as blk:
            for name in ENGS:
                mine = [ev for ev in evs if ev[0] == name]
                if not mine:
                    continue

                def mk(mine=mine):
                    def f(eng):
                        for _, waits, fn, incs in mine:
                            for s, v in waits:
                                eng.wait_ge(self.sems[s], v)
                            ins = fn(eng) if fn is not None else None
                            used = False
                            for s, n in incs:
                                if ins is not None and not used:
                                    ins.then_inc(self.sems[s], n)
                                    used = True
                                else:
                                    eng.sem_inc(self.sems[s], n)
                    return f

                getattr(blk, name)(mk())


def build_nc(stop_after=None, p2_level=4, q2_on=True, p2_ng=None, skip_exp=False, p2_pieces=None):
    nc = bass.Bass("TRN2", target_bir_lowering=False)

    xctxT = nc.dram_tensor("xctxT", [DM, CTX], F32, kind="ExternalInput").ap()
    wqT = nc.dram_tensor("wqT", [DM, DM], F32, kind="ExternalInput").ap()
    wkTd = nc.dram_tensor("wkTd", [DM, 512], F32, kind="ExternalInput").ap()
    wvT = nc.dram_tensor("wvT", [DM, 256], F32, kind="ExternalInput").ap()
    woT = nc.dram_tensor("woT", [DM, DM], F32, kind="ExternalInput").ap()
    w1T = nc.dram_tensor("w1T", [DM, DFF], F32, kind="ExternalInput").ap()
    w2T = nc.dram_tensor("w2T", [DFF, DM], F32, kind="ExternalInput").ap()
    b1c = nc.dram_tensor("b1c", [128, NFC], F32, kind="ExternalInput").ap()
    b2c = nc.dram_tensor("b2c", [128, DCH], F32, kind="ExternalInput").ap()
    trimask = nc.dram_tensor("trimask", [128, 1024], BF16, kind="ExternalInput").ap()
    identin = nc.dram_tensor("identin", [128, 128], BF16, kind="ExternalInput").ap()
    validrep = nc.dram_tensor("validrep", [CTX, 128], BF16, kind="ExternalInput").ap()
    outT = nc.dram_tensor("outT", [DM, QTOK], F32, kind="ExternalOutput").ap()
    w1q_t = nc.dram_tensor("w1q_t", [NFC, 128, DM], BF16).ap()
    r1d = nc.dram_tensor("r1d", [DM, QTOK], F32).ap()
    w2q_t = nc.dram_tensor("w2q_t", [DCH, 128, DFF], BF16).ap()
    meand = nc.dram_tensor("meand", [128, QTOK], F32).ap()
    rstdd = nc.dram_tensor("rstdd", [128, QTOK], F32).ap()

    es = ExitStack()
    with es:
        S = Sched(nc, es)
        sb = lambda name, shape, dt: es.enter_context(nc.sbuf_tensor(name, shape, dt))

        tri_sb = sb("tri_sb", [128, 1024], BF16)
        id_sb = sb("id_sb", [128, 128], BF16)
        b1_sb = sb("b1_sb", [128, NFC], F32)
        b2_sb = sb("b2_sb", [128, DCH], F32)
        ones_bf = sb("ones_bf", [128, 128], BF16)
        ones_f32 = sb("ones_f32", [128, 128], F32)
        qred = sb("qred", [128, 512], F32)
        gpart = sb("gpart", [128, 32], F32)
        gsum = sb("gsum", [128, 2], F32)
        th1 = sb("th1", [128, 1], F32)
        nth1 = sb("nth1", [128, 1], F32)
        th2 = sb("th2", [128, 1], F32)
        nth2 = sb("nth2", [128, 1], F32)
        g1col = sb("g1col", [128, 1], F32)
        g2col = sb("g2col", [128, 1], F32)
        cm30 = sb("cm30", [128, 1], F32)

        w1b = [es.enter_context(nc.sbuf_tensor(f"w1b{i}", [128, DM], BF16,
                                               side="right")) for i in range(3)]
        otn_ctx = ExitStack()
        OTn = [otn_ctx.enter_context(
            nc.sbuf_tensor(f"OTn{i}", [128, DCH * 512], BF16, side="right"))
            for i in range(2)]
        wo_bf = otn_ctx.enter_context(
            nc.sbuf_tensor("wo_bf", [128, DCH * 1024], BF16, side="right"))

        ld_small = 0
        gd_all = 0

        def _early_out(src_ap):
            v = S.dma("sync", outT.rearrange("(dc p) q -> p dc q", p=128)[:, 0, 0:512],
                      src_ap, "OST")
            S.do("sync", None, waits=(("OST", v),))

        with ExitStack() as p12:
            sb12 = lambda name, shape, dt: p12.enter_context(nc.sbuf_tensor(name, shape, dt))
            KTd = sb12("KTd", [128, 4 * CTX], BF16)
            Vv = sb12("Vv", [128, CTXCH * 256], BF16)
            QT = sb12("QT", [128, DCH * QTOK], BF16)
            vrep_sb = sb12("vrep_sb", [128, CTXCH * 128], BF16)
            qin = [sb12(f"qin{i}", [128, 1024], F32) for i in range(4)]
            qx = sb12("qx", [128, 1024], BF16)
            qout = [sb12(f"qout{i}", [128, 1024], BF16) for i in range(2)]
            mean_all = sb12("mean_all", [128, QTOK], F32)
            rstd_all = sb12("rstd_all", [128, QTOK], F32)

            with ExitStack() as p1:
                sb1 = lambda name, shape, dt: p1.enter_context(nc.sbuf_tensor(name, shape, dt))
                ps1 = lambda name: p1.enter_context(nc.psum_tensor(name, [128, 512], F32))
                xb = [sb1(f"xb{i}", [128, DCH * 512], BF16) for i in range(2)]
                wq_bf = sb1("wq_bf", [128, DCH * 1024], BF16)
                wkd_bf = sb1("wkd_bf", [128, DCH * 512], BF16)
                wv_bf = sb1("wv_bf", [128, DCH * 256], BF16)
                kps = [ps1(f"kps{i}") for i in range(2)]
                vps = [ps1(f"vps{i}") for i in range(2)]
                qps = [ps1(f"qps{i}") for i in range(2)]

                S.dma("sync", tri_sb[:, :], trimask, "LDS")
                S.dma("sync", id_sb[:, :], identin, "LDS")
                S.dma("sync", b1_sb[:, :], b1c, "LDS")
                S.dma("sync", b2_sb[:, :], b2c, "LDS")
                S.dma("sync", vrep_sb[:].rearrange("p (ca j) -> p ca j", ca=CTXCH),
                      validrep.rearrange("(ca p) j -> p ca j", p=128), "LDS")
                ld_small = S.c["LDS"]
                S.do("vector", lambda e: e.memset(ones_bf[:, :], 1.0), incs=(("V", 1),))
                S.do("vector", lambda e: e.memset(ones_f32[:, :], 1.0), incs=(("V", 1),))
                S.do("vector", lambda e: e.memset(cm30[:, :], -30.0), incs=(("V", 1),))
                v_consts = S.c["V"]

                xctxT_r = xctxT.rearrange("(dc p) c -> p dc c", p=128)

                def emit_xb_dma(cb, wait_t):
                    dst = xb[cb % 2][:].rearrange("p (dc c) -> p dc c", dc=DCH)
                    src = xctxT_r[:, :, cb * 512:(cb + 1) * 512]
                    waits = (("T", wait_t),) if wait_t > 0 else ()
                    return S.dma("gpsimd", dst, src, f"XB{cb % 2}", waits=waits)

                gw_k = S.dma("gpsimd", wkd_bf[:].rearrange("p (dc f) -> p dc f", dc=DCH),
                             wkTd.rearrange("(dc p) f -> p dc f", p=128), "GW")

                w1T_r = w1T.rearrange("(dc p) f -> p dc f", p=128)
                w2T_r = w2T.rearrange("(a p) d -> p a d", p=128)
                p1_steps = []
                for ci in range(32):
                    dc, fq = ci // 4, ci % 4
                    p1_steps.append((w1T_r[:, dc, fq * 1024:(fq + 1) * 1024], 1))
                for ci in range(32):
                    p1_steps.append((w2T_r[:, ci, :], 1))

                NQ1 = 64
                q1_dma_done = [0] * NQ1
                q1_red_done = [0] * NQ1
                st1 = {"issued": 0, "reduced": 0}

                def q1_issue():
                    i = st1["issued"]
                    if i >= NQ1:
                        return
                    if i >= 4:
                        assert q1_red_done[i - 4] > 0
                        waits = (("V", q1_red_done[i - 4]),)
                    else:
                        waits = ()
                    src, nsub = p1_steps[i]
                    dst = qin[i % 4][:, :]
                    if nsub > 1:
                        dst = dst.rearrange("p (a d) -> p a d", a=nsub)
                    q1_dma_done[i] = S.dma("sync", dst, src, f"QI{i % 4}", waits=waits)
                    st1["issued"] += 1

                def q1_reduce():
                    i = st1["reduced"]
                    if i >= st1["issued"] or i >= NQ1:
                        return
                    S.do("vector",
                         lambda e, i=i: e.tensor_reduce(
                             qred[:, i * 8:(i + 1) * 8],
                             qin[i % 4][:].rearrange("p (a x) -> p a x", a=8),
                             AX.X, ALU.add, apply_absolute_value=True),
                         waits=((f"QI{i % 4}", q1_dma_done[i]),), incs=(("V", 1),))
                    q1_red_done[i] = S.c["V"]
                    st1["reduced"] += 1

                for _ in range(4):
                    q1_issue()

                xb_dma_done = [0] * 8
                kps_evac = [0, 0]
                vps_evac = [0, 0]
                qps_evac = [0, 0]
                xb_dma_done[0] = emit_xb_dma(0, 0)
                gw_v = S.dma("gpsimd", wv_bf[:].rearrange("p (dc f) -> p dc f", dc=DCH),
                             wvT.rearrange("(dc p) f -> p dc f", p=128), "GW")
                xb_dma_done[1] = emit_xb_dma(1, 0)
                gw_q = S.dma("gpsimd", wq_bf[:].rearrange("p (dc f) -> p dc f", dc=DCH),
                             wqT.rearrange("(dc p) f -> p dc f", p=128), "GW")

                for cb in range(6):
                    x = xb[cb % 2]
                    first_wait = ((f"XB{cb % 2}", xb_dma_done[cb]), ("GW", gw_k))
                    for fc in range(4):
                        slot = fc % 2
                        w = (("A", kps_evac[slot]),) + (first_wait if fc < 2 else ())
                        for dc in range(DCH):
                            S.do("tensor",
                                 lambda e, dc=dc, fc=fc, x=x, slot=slot: e.matmul(
                                     kps[slot][:, :],
                                     wkd_bf[:, dc * 512 + fc * 128: dc * 512 + fc * 128 + 128],
                                     x[:, dc * 512:(dc + 1) * 512],
                                     start=(dc == 0), stop=(dc == DCH - 1)),
                                 waits=(w if dc == 0 else ()),
                                 incs=((("T", 1),) if dc == DCH - 1 else ()))
                        t_done = S.c["T"]
                        S.do("scalar",
                             lambda e, fc=fc, cb=cb, slot=slot: e.copy(
                                 KTd[:, fc * CTX + cb * 512: fc * CTX + cb * 512 + 512],
                                 kps[slot][:, :]),
                             waits=(("T", t_done),), incs=(("A", 1),))
                        kps_evac[slot] = S.c["A"]
                    for tc in range(4):
                        slot = tc % 2
                        w = (("A", vps_evac[slot]),) + ((("GW", gw_v),) if cb == 0 and tc == 0 else ())
                        for dc in range(DCH):
                            S.do("tensor",
                                 lambda e, dc=dc, tc=tc, x=x, slot=slot: e.matmul(
                                     vps[slot][:, 0:256],
                                     x[:, dc * 512 + tc * 128: dc * 512 + tc * 128 + 128],
                                     wv_bf[:, dc * 256:(dc + 1) * 256],
                                     start=(dc == 0), stop=(dc == DCH - 1)),
                                 waits=(w if dc == 0 else ()),
                                 incs=((("T", 1),) if dc == DCH - 1 else ()))
                        t_done = S.c["T"]
                        S.do("scalar",
                             lambda e, cb=cb, tc=tc, slot=slot: e.copy(
                                 Vv[:, (cb * 4 + tc) * 256:(cb * 4 + tc) * 256 + 256],
                                 vps[slot][:, 0:256]),
                             waits=(("T", t_done),), incs=(("A", 1),))
                        vps_evac[slot] = S.c["A"]
                        q1_reduce()
                        q1_issue()
                    if cb >= 2:
                        qcol = (cb - 2) * 512
                        for fc in range(DCH):
                            slot = fc % 2
                            w = (("A", qps_evac[slot]),) + ((("GW", gw_q),) if cb == 2 and fc == 0 else ())
                            for dc in range(DCH):
                                S.do("tensor",
                                     lambda e, dc=dc, fc=fc, x=x, slot=slot: e.matmul(
                                         qps[slot][:, :],
                                         wq_bf[:, dc * 1024 + fc * 128: dc * 1024 + fc * 128 + 128],
                                         x[:, dc * 512:(dc + 1) * 512],
                                         start=(dc == 0), stop=(dc == DCH - 1)),
                                     waits=(w if dc == 0 else ()),
                                     incs=((("T", 1),) if dc == DCH - 1 else ()))
                            t_done = S.c["T"]
                            S.do("scalar",
                                 lambda e, fc=fc, qcol=qcol, slot=slot: e.copy(
                                     QT[:, fc * QTOK + qcol: fc * QTOK + qcol + 512],
                                     qps[slot][:, :]),
                                 waits=(("T", t_done),), incs=(("A", 1),))
                            qps_evac[slot] = S.c["A"]
                            q1_reduce()
                            q1_issue()
                    if cb + 2 < 6:
                        xb_dma_done[cb + 2] = emit_xb_dma(cb + 2, S.c["T"])

                S.barrier(drain=("XB0", "XB1", "GW"))
                if stop_after == "P1":
                    _early_out(qred[:, :])
                S.flush()
            if stop_after == "P1":
                return nc

            with ExitStack() as p2:
                sb2 = lambda name, shape, dt: p2.enter_context(nc.sbuf_tensor(name, shape, dt))
                Tst = sb2("Tst", [128, 4608], F32)
                Pst = [sb2(f"Pst{i}", [128, 4608], BF16) for i in range(2)]
                den_sb = sb2("den_sb", [128, 512], F32)
                spsP = [p2.enter_context(nc.psum_tensor(f"spsP{i}", [128, 1024], F32))
                        for i in range(2)]
                pvd = p2.enter_context(nc.psum_tensor("pvd", [128, 512], F32))
                wopsb = [p2.enter_context(nc.psum_tensor(f"wops{i}", [128, 512],
                                                         F32)) for i in range(2)]
                lnst = p2.enter_context(nc.psum_tensor("lnst", [128, 512], F32))
                xr = [sb2(f"xr{i}", [128, 256], F32) for i in range(2)]
                r1f = [sb2(f"r1f{i}", [128, 256], F32) for i in range(2)]
                comb = [sb2(f"comb{i}", [128, 512], BF16) for i in range(4)]

                w1T_q = w1T.rearrange("(dc p) f -> p dc f", p=128)
                w2T_q = w2T.rearrange("(fq fs p) d -> p fq fs d", fq=4, p=128)
                NQ2 = 64
                q2 = [("w1", fc) for fc in range(NFC)] + [("w2", i) for i in range(32)]
                q2_dma_done = [0] * NQ2
                q2_cmp_done = [0] * NQ2
                q2_out_done = [0] * NQ2
                st2 = {"issued": 0, "done": 0}

                def q2_issue():
                    i = st2["issued"]
                    if i >= NQ2 or not q2_on:
                        return
                    kind, idx = q2[i]
                    if kind == "w1":
                        src = w1T_q[:, :, idx * 128:(idx + 1) * 128]
                        dst = qin[i % 4][:].rearrange("p (a f) -> p a f", a=DCH)
                    else:
                        fq, dc = idx // 8, idx % 8
                        src = w2T_q[:, fq, :, dc * 128:(dc + 1) * 128]
                        dst = qin[i % 4][:].rearrange("p (a f) -> p a f", a=8)
                    if i >= 4:
                        assert q2_cmp_done[i - 4] > 0
                        waits = (("V", q2_cmp_done[i - 4]),)
                    else:
                        assert v_th_box[0] > 0
                        waits = (("V", v_th_box[0]),)
                    q2_dma_done[i] = S.dma("sync", dst, src, f"QI{i % 4}", waits=waits)
                    st2["issued"] += 1

                def q2_step():
                    i = st2["done"]
                    if i >= st2["issued"] or i >= NQ2:
                        return
                    kind, idx = q2[i]
                    neg = nth1 if kind == "w1" else nth2
                    pos = th1 if kind == "w1" else th2
                    waits = [(f"QI{i % 4}", q2_dma_done[i])]
                    if i >= 2:
                        waits.append((f"QO{i % 2}", q2_out_done[i - 2]))
                    S.do("vector",
                         lambda e, i=i, neg=neg: e.tensor_single_scalar(
                             qx[:, :], qin[i % 4][:, :], neg[:, :], ALU.is_lt),
                         waits=tuple(waits), incs=(("V", 1),))
                    S.do("vector",
                         lambda e, i=i, pos=pos: e.scalar_tensor_tensor(
                             qout[i % 2][:, :], qin[i % 4][:, :], pos[:, :],
                             qx[:, :], ALU.is_gt, ALU.subtract),
                         incs=(("V", 1),))
                    q2_cmp_done[i] = S.c["V"]
                    if kind == "w1":
                        dst = w1q_t[idx]
                    else:
                        fq, dc = idx // 8, idx % 8
                        dst = w2q_t[dc][:, fq * 1024:(fq + 1) * 1024]
                    q2_out_done[i] = S.dma(
                        "gpsimd", dst, qout[i % 2][:, :], f"QO{i % 2}",
                        waits=(("V", q2_cmp_done[i]),))
                    st2["done"] += 1

                v_th_box = [0]

                def emit_gamma():
                    S.do("vector", lambda e: e.tensor_reduce(
                        gpart[:, 0:16],
                        qred[:, 0:256].rearrange("p (a x) -> p a x", a=16),
                        AX.X, ALU.add),
                        waits=(("V", q1_red_done[31]),), incs=(("V", 1),))
                    S.do("vector", lambda e: e.tensor_reduce(
                        gpart[:, 16:32],
                        qred[:, 256:512].rearrange("p (a x) -> p a x", a=16),
                        AX.X, ALU.add),
                        waits=(("V", q1_red_done[63]),), incs=(("V", 1),))
                    v_g = S.c["V"]
                    S.do("tensor", lambda e: e.matmul(
                        lnst[:, 0:16], ones_f32[:, :], gpart[:, 0:16],
                        start=True, stop=True),
                        waits=(("V", v_g),), incs=(("T", 1),))
                    S.do("tensor", lambda e: e.matmul(
                        lnst[:, 16:32], ones_f32[:, :], gpart[:, 16:32],
                        start=True, stop=True), incs=(("T", 1),))
                    t_g = S.c["T"]
                    S.do("vector", lambda e: e.tensor_reduce(
                        gsum[:, 0:1], lnst[:, 0:16], AX.X, ALU.add),
                        waits=(("T", t_g),), incs=(("V", 1),))
                    S.do("vector", lambda e: e.tensor_reduce(
                        gsum[:, 1:2], lnst[:, 16:32], AX.X, ALU.add),
                        incs=(("V", 1),))
                    S.do("vector", lambda e: e.tensor_scalar(
                        th1[:, :], gsum[:, 0:1], 0.5 / QCNT, 0.5 * EPS,
                        ALU.mult, ALU.add), incs=(("V", 1),))
                    S.do("vector", lambda e: e.tensor_scalar(
                        th2[:, :], gsum[:, 1:2], 0.5 / QCNT, 0.5 * EPS,
                        ALU.mult, ALU.add), incs=(("V", 1),))
                    S.do("vector", lambda e: e.tensor_single_scalar(
                        nth1[:, :], th1[:, :], -1.0, ALU.mult), incs=(("V", 1),))
                    S.do("vector", lambda e: e.tensor_single_scalar(
                        nth2[:, :], th2[:, :], -1.0, ALU.mult), incs=(("V", 1),))
                    v_th_box[0] = S.c["V"]

                def quant_pump(g):
                    if st1["reduced"] < NQ1:
                        for _ in range(2):
                            q1_reduce()
                            q1_issue()
                        if st1["reduced"] >= NQ1:
                            emit_gamma()
                        return
                    if v_th_box[0] == 0:
                        emit_gamma()
                    q2_step()
                    q2_issue()
                    if st2["done"] < g - 6:
                        q2_step()
                        q2_issue()

                Pst_r = [p[:].rearrange("p (rel hp par x) -> p par rel hp x",
                                        rel=NREL, hp=2, par=2) for p in Pst]
                OTn_rv = [o[:].rearrange("p (ch q4 x) -> p ch q4 x", ch=DCH, q4=4)
                          for o in OTn]

                piece_T = {}
                mask_V = {}
                tanh_done = {}
                exp_done = {}
                pv_done = {}
                norm_done = {}
                NG = NQT * 4

                def stg_off(kb, par):
                    if kb < 8:
                        return (kb // 2) * 1024 + par * 512 + (kb % 2) * 256
                    return 4096 + par * 256

                tanhP = {}

                def emit_scores_acts(g):
                    qt, kv = divmod(g, 4)
                    for kbp in range(5):
                        na = 2 if kbp < 4 else 1
                        edge = kbp == 0 or kbp == 4
                        ps = spsP[(5 * g + kbp) % 2]
                        for a in range(na):
                            kb = 2 * kbp + a
                            ca = qt + kb
                            for he in range(2):
                                for par in range(2):
                                    base = par * 64
                                    lhsT = KTd[base:base + 64,
                                               kv * CTX + ca * 128:
                                               kv * CTX + ca * 128 + 128]
                                    hg = kv * 4 + 2 * he + par
                                    col = par * 512 + a * 256 + he * 128
                                    rhs = QT[base:base + 64,
                                             (hg // 2) * QTOK + qt * 128:
                                             (hg // 2) * QTOK + qt * 128 + 128]
                                    waits = ()
                                    if par == 0 and a == 0 and he == 0:
                                        pp = 5 * g + kbp - 2
                                        prev = divmod(pp, 5) if pp >= 0 else None
                                        if prev in tanhP:
                                            waits = (("A", tanhP[prev]),)
                                    first_in_bank = (a == 0) and (he == 0)
                                    last_in_bank = (a == na - 1) and (he == 1) \
                                        and not edge
                                    piece_last = last_in_bank and par == 1
                                    S.do("tensor",
                                         lambda e, ps=ps, col=col, lhsT=lhsT,
                                         rhs=rhs, st=first_in_bank,
                                         sp=last_in_bank: e.matmul(
                                             ps[:, col:col + 128], lhsT, rhs,
                                             start=st, stop=sp),
                                         waits=waits,
                                         incs=((("T", 1),) if piece_last else ()))
                        if edge:
                            moff = 0 if kbp == 0 else 512
                            for par in range(2):
                                S.do("tensor",
                                     lambda e, ps=ps, par=par, moff=moff: e.matmul(
                                         ps[:, par * 512: par * 512 + 256],
                                         id_sb[:, :], tri_sb[:, moff:moff + 256],
                                         start=False, stop=True),
                                     incs=((("T", 1),) if par == 1 else ()))
                        t_piece = S.c["T"]
                        w = [("T", t_piece)]
                        if kbp < 4:
                            S.do("scalar",
                                 lambda e, ps=ps, kbp=kbp: e.activation(
                                     Tst[:, kbp * 1024:(kbp + 1) * 1024],
                                     ps[:, 0:1024], AF.Tanh, scale=1.0 / 240.0),
                                 waits=tuple(w), incs=(("A", 1),))
                        else:
                            S.do("scalar",
                                 lambda e, ps=ps: e.activation(
                                     Tst[:, 4096:4608],
                                     ps[:].rearrange("p (par x) -> p par x",
                                                     par=2)[:, :, 0:256],
                                     AF.Tanh, scale=1.0 / 240.0),
                                 waits=tuple(w), incs=(("A", 1),))
                        tanhP[(g, kbp)] = S.c["A"]
                    w = [("A", tanhP[(g, 4)])]
                    if g >= 2 and pv_done.get(g - 2, 0) > 0:
                        w.append(("T", pv_done[g - 2]))
                    if g == 0:
                        w.append(("V", v_consts), )
                    S.do("scalar",
                         lambda e, g=g: e.activation(
                             Pst[g % 2][:, :], Tst[:, :], AF.Exp, bias=cm30[:, :],
                             scale=30.0),
                         waits=tuple(w), incs=(("A", 1),))
                    exp_done[g] = S.c["A"]

                def emit_pv(g):
                    qt, kv = divmod(g, 4)
                    P = Pst[g % 2]
                    pd = pvd
                    for rel in range(NREL):
                        ca = qt + rel
                        vt = Vv[:, ca * 256 + kv * 64: ca * 256 + kv * 64 + 64]
                        for par in range(2):
                            w0 = []
                            if par == 0 and rel == 0:
                                w0.append(("A", exp_done[g]))
                                if g >= 1 and norm_done.get(g - 1, 0) > 0:
                                    w0.append(("V", norm_done[g - 1]))
                            tp = None if par == 0 else (0, 64)
                            out = pd[0:64, 0:256] if par == 0 else pd[64:128, 0:256]
                            rhs = P[:, stg_off(rel, par): stg_off(rel, par) + 256]
                            S.do("tensor",
                                 lambda e, out=out, vt=vt, rhs=rhs, rel=rel,
                                 tp=tp: e.matmul(
                                     out, vt, rhs, start=(rel == 0),
                                     stop=(rel == NREL - 1), tile_position=tp,
                                     skip_group_check=True),
                                 waits=tuple(w0))
                    for rel in range(NREL):
                        ca = qt + rel
                        for par in range(2):
                            rhs = P[:, stg_off(rel, par): stg_off(rel, par) + 256]
                            w0 = []
                            if par == 0 and rel == 0 and g == 0:
                                w0.append(("LDS", ld_small))
                            tp = None if par == 0 else (0, 64)
                            out = (pd[0:64, 256:512] if par == 0
                                   else pd[64:128, 256:512])
                            S.do("tensor",
                                 lambda e, ca=ca, rhs=rhs, rel=rel, out=out,
                                 tp=tp: e.matmul(
                                     out, vrep_sb[:, ca * 128: ca * 128 + 64],
                                     rhs, start=(rel == 0),
                                     stop=(rel == NREL - 1), tile_position=tp,
                                     skip_group_check=True),
                                 waits=tuple(w0),
                                 incs=((("T", 1),)
                                       if rel == NREL - 1 and par == 1 else ()))
                    pv_done[g] = S.c["T"]

                def emit_norm(g):
                    qt, kv = divmod(g, 4)
                    pd = pvd
                    w0 = [("T", pv_done[g])]
                    if g % 16 == 0 and (g // 16 - 2) in win_last_T:
                        w0.append(("T", win_last_T[g // 16 - 2]))
                    S.do("vector",
                         lambda e, pd=pd: e.reciprocal(den_sb[:, 0:256],
                                                       pd[:, 256:512]),
                         waits=tuple(w0), incs=(("V", 1),))
                    vr = S.c["V"]
                    slot = (qt // 4) % 2
                    for par in range(2):
                        S.do("vector",
                             lambda e, par=par, kv=kv, qt=qt, pd=pd,
                             slot=slot: e.tensor_tensor(
                                 OTn_rv[slot][par * 64:par * 64 + 64,
                                              2 * kv:2 * kv + 2, qt % 4],
                                 pd[par * 64:par * 64 + 64, 0:256],
                                 den_sb[par * 64:par * 64 + 64, 0:256],
                                 ALU.mult),
                             waits=((("V", vr),) if par == 0 else ()),
                             incs=((("V", 1),) if par == 1 else ()))
                    norm_done[g] = S.c["V"]

                fold_on = p2_ng is None
                woT_r = woT.rearrange("(dc p) f -> p dc f", p=128)
                wo_bf_r = wo_bf[:].rearrange("p (dc f) -> p dc f", dc=DCH)
                r1d_r = r1d.rearrange("(dc p) q -> p dc q", p=128)
                wo_gd = [0]
                evac_V = [0] * 64
                stats_T = [0] * 64
                t_mm_arr = [0] * 64
                r1s_cnt = [0] * 64
                xr_cnt = [0] * 64
                xr_issued = [False] * 64
                last_extr_V = [0]
                win_last_T = {}
                w1_pre = [0, 0]
                mrs_after = {}

                def xr_buf(u):
                    s = u % 4
                    if s < 2:
                        return xr[s][:, :]
                    return qred[:, (s - 2) * 256:(s - 1) * 256]

                def ensure_xr(u):
                    if u >= 64 or xr_issued[u]:
                        return
                    k, r = u // 16, u % 16
                    half, hc = r // 8, r % 8
                    c0 = k * 512 + half * 256
                    w = []
                    if u >= 4:
                        assert evac_V[u - 4] > 0
                        w.append(("V", evac_V[u - 4]))
                    elif u in (2, 3):
                        w.append(("V", v_th_box[0]))
                    xr_cnt[u] = S.dma(
                        "sync", xr_buf(u),
                        xctxT_r[:, hc, WINDOW + c0: WINDOW + c0 + 256],
                        f"XR{u % 4}", waits=tuple(w))
                    xr_issued[u] = True

                v_comb_arr = [0] * 64

                def wo_stageA(u):
                    k, r = u // 16, u % 16
                    half, hc = r // 8, r % 8
                    wop = wopsb[u % 2]
                    for jc in range(DCH):
                        w = []
                        if jc == 0:
                            w.append(("V", norm_done[k * 16 + 15]))
                            w.append(("GW", wo_gd[0]))
                            if u >= 2:
                                w.append(("V", evac_V[u - 2]))
                        S.do("tensor",
                             lambda e, jc=jc, hc=hc, k=k, half=half,
                             wop=wop: e.matmul(
                                 wop[:, 0:256],
                                 wo_bf[:, jc * 1024 + hc * 128:
                                       jc * 1024 + hc * 128 + 128],
                                 OTn[k % 2][:, jc * 512 + half * 256:
                                            jc * 512 + half * 256 + 256],
                                 start=(jc == 0), stop=(jc == DCH - 1)),
                             waits=tuple(w) if jc == 0 else (),
                             incs=((("T", 1),) if jc == DCH - 1 else ()))
                    stats_T[u] = 0
                    win_last_T[k] = S.c["T"]
                    t_mm_arr[u] = S.c["T"]

                def wo_stageB(u):
                    k, r = u // 16, u % 16
                    half, hc = r // 8, r % 8
                    c0 = k * 512 + half * 256
                    w = [("T", t_mm_arr[u]), (f"XR{u % 4}", xr_cnt[u])]
                    if u >= 2:
                        w.append((f"R1S{u % 2}", r1s_cnt[u - 2]))
                    S.do("vector", lambda e, u=u: e.tensor_tensor(
                        r1f[u % 2][:, :], wopsb[u % 2][:, 0:256],
                        xr_buf(u), ALU.add),
                        waits=tuple(w), incs=(("V", 1),))
                    evac_V[u] = S.c["V"]
                    r1s_cnt[u] = S.dma(
                        "gpsimd", r1d_r[:, hc, c0:c0 + 256], r1f[u % 2][:, :],
                        f"R1S{u % 2}", waits=(("V", evac_V[u]),))
                    w = []
                    if u >= 4:
                        w.append(("T", stats_T[u - 4]))
                    S.do("vector", lambda e, u=u: e.tensor_copy(
                        comb[u % 4][:, 0:256], r1f[u % 2][:, :]),
                        waits=tuple(w), incs=(("V", 1),))
                    S.do("vector", lambda e, u=u: e.tensor_tensor(
                        comb[u % 4][:, 256:512], r1f[u % 2][:, :],
                        r1f[u % 2][:, :], ALU.mult), incs=(("V", 1),))
                    v_comb_arr[u] = S.c["V"]

                def wo_stageC(u):
                    hc = u % 8
                    w = [("V", v_comb_arr[u])]
                    if hc == 0 and last_extr_V[0] > 0:
                        w.append(("V", last_extr_V[0]))
                    S.do("tensor", lambda e, u=u, hc=hc: e.matmul(
                        lnst[:, :], ones_bf[:, :], comb[u % 4][:, :],
                        start=(hc == 0), stop=(hc == DCH - 1)),
                        waits=tuple(w), incs=(("T", 1),))
                    stats_T[u] = S.c["T"]

                def wo_stageD(u):
                    k, r = u // 16, u % 16
                    half = r // 8
                    c0 = k * 512 + half * 256
                    S.do("vector", lambda e, c0=c0: e.tensor_single_scalar(
                        mean_all[:, c0:c0 + 256], lnst[:, 0:256],
                        1.0 / DM, ALU.mult),
                        waits=(("T", stats_T[u]),), incs=(("V", 1),))
                    S.do("vector", lambda e, c0=c0: e.tensor_tensor(
                        den_sb[:, 256:512], mean_all[:, c0:c0 + 256],
                        mean_all[:, c0:c0 + 256], ALU.mult), incs=(("V", 1),))
                    S.do("vector", lambda e, c0=c0: e.scalar_tensor_tensor(
                        rstd_all[:, c0:c0 + 256], lnst[:, 256:512], 1.0 / DM,
                        den_sb[:, 256:512], ALU.mult, ALU.subtract),
                        incs=(("V", 1),))
                    last_extr_V[0] = S.c["V"]
                    S.dma("gpsimd", meand[:, c0:c0 + 256],
                          mean_all[:, c0:c0 + 256], "MRS",
                          waits=(("V", last_extr_V[0]),))
                    S.dma("gpsimd", rstdd[:, c0:c0 + 256],
                          rstd_all[:, c0:c0 + 256], "MRS",
                          waits=(("V", last_extr_V[0]),))
                    mrs_after[(k, half)] = S.c["MRS"]

                def wo_pump(i):
                    uD = i - 20
                    if 0 <= uD < 64 and uD % 8 == 7:
                        wo_stageD(uD)
                    uC = i - 19
                    if 0 <= uC < 64:
                        wo_stageC(uC)
                    uB = i - 18
                    if 0 <= uB < 64:
                        wo_stageB(uB)

                def wo_pump_a(i):
                    uA = i - 17
                    if 0 <= uA < 64:
                        ensure_xr(uA)
                        ensure_xr(uA + 1)
                        ensure_xr(uA + 2)
                        ensure_xr(uA + 3)
                        wo_stageA(uA)

                for g in range(NG if p2_ng is None else p2_ng):
                    if fold_on and g < DCH:
                        wo_gd[0] = S.dma("gpsimd", wo_bf_r[:, g, :],
                                         woT_r[:, g, :], "GW")
                    quant_pump(g)
                    if fold_on:
                        wo_pump(g)
                    emit_scores_acts(g)
                    if g >= 1:
                        emit_pv(g - 1)
                        emit_norm(g - 1)
                    if fold_on:
                        wo_pump_a(g)
                _ng = NG if p2_ng is None else p2_ng
                emit_pv(_ng - 1)
                emit_norm(_ng - 1)
                if fold_on:
                    for i in range(_ng, _ng + 22):
                        wo_pump(i)
                        wo_pump_a(i)
                while st1["reduced"] < NQ1:
                    q1_reduce()
                    q1_issue()
                if v_th_box[0] == 0:
                    emit_gamma()
                while q2_on and st2["done"] < NQ2:
                    q2_step()
                    q2_issue()
                qo_final = (S.c["QO0"], S.c["QO1"])

        otn_ctx.close()
        x2 = sb("x2", [128, DCH * QTOK], F32)
        mean4 = sb("mean4", [128, 1024], F32)
        rstd4 = sb("rstd4", [128, 1024], F32)
        f_tmp = sb("f_tmp", [128, 512], F32)
        sig_sb = sb("sig_sb4", [128, 512], F32)
        x2bf = sb("x2bf", [128, DCH * 1024], BF16)
        x2_r = x2[:].rearrange("p (dc c) -> p dc c", dc=DCH)
        BDRAIN = ("QI0", "QI1", "QI2", "QI3", "QO0", "QO1", "LDS", "GW",
                  "XR0", "XR1", "XR2", "XR3", "MRS", "R1S0", "R1S1")

        if stop_after == "P2":
            S.barrier(drain=BDRAIN)
            _early_out(Tst[:, 0:512] if p2_level < 4 else den_sb[:, :])
            S.flush()
            return nc

        pvT = pv_done[_ng - 1]
        S.dma("sync", mean4[:, :], meand[:, 0:1024], "MRL",
              waits=(("T", pvT), ("MRS", mrs_after[(1, 1)])))
        mrl_cnt = [0, 0]
        mrl_cnt[0] = S.dma("sync", rstd4[:, :], rstdd[:, 0:1024], "MRL")
        rl_tot = {}
        for t in (0, 1):
            for dc in range(DCH):
                eng, q = (("sync", "RL") if (t == 0 and dc % 2 == 0)
                          else ("gpsimd", "RLG"))
                w = ()
                if dc == 0 or (t == 0 and dc == 1):
                    w = (("T", pvT),
                         ("R1S0", r1s_cnt[30 if t == 0 else 62]),
                         ("R1S1", r1s_cnt[31 if t == 0 else 63]))
                S.dma(eng, x2_r[:, dc, t * 1024:(t + 1) * 1024],
                      r1d_r[:, dc, t * 1024:(t + 1) * 1024], q, waits=w)
            rl_tot[t] = (S.c["RL"], S.c["RLG"])

        g_norm = [0, 0]

        def emit_norm_cast(t, wait_t):
            for ch in range(2):
                w = []
                if ch == 0:
                    w = [("MRL", mrl_cnt[t])]
                    if t == 0:
                        w += [("T", pvT), ("V", q2_cmp_done[NQ2 - 1])]
                S.do("vector", lambda e, ch=ch: e.tensor_single_scalar(
                    f_tmp[:, :], rstd4[:, ch * 512:(ch + 1) * 512],
                    EPS, ALU.add), waits=tuple(w), incs=(("V", 1),))
                v_a = S.c["V"]
                S.do("scalar", lambda e: e.activation(
                    sig_sb[:, :], f_tmp[:, :], AF.Sqrt),
                    waits=(("V", v_a),), incs=(("A", 1),))
                a_s = S.c["A"]
                S.do("vector", lambda e, ch=ch: e.reciprocal(
                    rstd4[:, ch * 512:(ch + 1) * 512], sig_sb[:, :]),
                    waits=(("A", a_s),), incs=(("V", 1),))
            v_conv = S.c["V"]
            eng = "vector" if t == 0 else "gpsimd"
            esem = "V" if t == 0 else "G"
            tmp = f_tmp if t == 0 else tmp_g
            first = True
            for dc in range(DCH):
                for h2 in range(2):
                    c0 = dc * QTOK + t * 1024 + h2 * 512
                    mc = h2 * 512
                    w = []
                    if first:
                        w = [("RL", rl_tot[t][0]), ("RLG", rl_tot[t][1]),
                             ("MRL", mrl_cnt[t])]
                        if eng == "gpsimd":
                            w.append(("V", v_conv))
                        first = False
                    S.do(eng, lambda e, c0=c0, mc=mc, tmp=tmp: e.tensor_tensor(
                        tmp[:, :], x2[:, c0:c0 + 512], mean4[:, mc:mc + 512],
                        ALU.subtract), waits=tuple(w), incs=((esem, 1),))
                    S.do(eng, lambda e, c0=c0, mc=mc, tmp=tmp: e.tensor_tensor(
                        x2[:, c0:c0 + 512], tmp[:, :], rstd4[:, mc:mc + 512],
                        ALU.mult), incs=((esem, 1),))
                ecnt = S.c[esem]
                wc = []
                if eng == "gpsimd":
                    wc.append(("G", ecnt))
                if dc == 0 and wait_t > 0:
                    wc.append(("T", wait_t))
                S.do("vector", lambda e, dc=dc, t=t: e.tensor_copy(
                    x2bf[:, dc * 1024:(dc + 1) * 1024],
                    x2[:, dc * QTOK + t * 1024: dc * QTOK + t * 1024 + 1024]),
                    waits=tuple(wc), incs=(("V", 1),))
            g_norm[t] = S.c[esem]
            return S.c["V"]

        cast_ready = emit_norm_cast(0, 0)
        for fc in range(2):
            w1_pre[fc] = S.dma(
                "sync", w1b[fc][:, :], w1q_t[fc], f"W1{fc}",
                waits=(("QO0", qo_final[0]), ("QO1", qo_final[1])))

        S.barrier(drain=BDRAIN)
        S.flush()

        with ExitStack() as p4:
            sb4 = lambda name, shape, dt: p4.enter_context(nc.sbuf_tensor(name, shape, dt))
            hT = sb4("hT", [128, NFC * 1024], BF16)
            w2b = [sb4(f"w2b{i}", [128, DFF], BF16) for i in range(3)]
            t_bf = sb4("t_bf4", [128, 512], BF16)
            t_sq = sb4("t_sq4", [128, 512], BF16)
            mean_sb = sb4("mean_sb4", [128, 512], F32)
            tmp_a = sb4("tmp_a4", [128, 512], F32)
            tmp_b = sb4("tmp_b4", [128, 512], F32)
            rstd_sb = sb4("rstd_sb4", [128, 512], F32)
            tmp_g = sb4("tmp_g4", [128, 512], F32)
            f1ps = [p4.enter_context(nc.psum_tensor(f"f1ps{i}", [128, 512], F32))
                    for i in range(2)]
            f2ps = [p4.enter_context(nc.psum_tensor(f"f2ps{i}", [128, 512], F32))
                    for i in range(2)]
            sum_ps = [p4.enter_context(nc.psum_tensor(f"sum_ps4{i}", [128, 512], F32))
                      for i in range(2)]
            sq_ps = [p4.enter_context(nc.psum_tensor(f"sq_ps4{i}", [128, 512], F32))
                     for i in range(2)]

            w1_dma = [0] * NFC
            w2_dma = [0] * (2 * DCH)
            w1_use = [0, 0, 0]
            w2_use = [0, 0, 0]
            relu_use = [0, 0]
            f2_evac = [0, 0]
            stats_guard4 = [0, 0]
            prev_stats4 = [0]

            g_guard4 = [0]

            def ln_tail4(slot, col0, stats_wait):
                sm, sq = sum_ps[slot], sq_ps[slot]
                w0 = tuple(stats_wait)
                if g_guard4[0] > 0:
                    w0 = w0 + (("G", g_guard4[0]),)
                S.do("vector", lambda e: e.tensor_single_scalar(
                    mean_sb[:, :], sm[:, :], 1.0 / DM, ALU.mult),
                    waits=w0, incs=(("V", 1),))
                S.do("vector", lambda e: e.tensor_tensor(
                    tmp_a[:, :], mean_sb[:, :], mean_sb[:, :], ALU.mult),
                    incs=(("V", 1),))
                S.do("vector", lambda e: e.scalar_tensor_tensor(
                    tmp_b[:, :], sq[:, :], 1.0 / DM, tmp_a[:, :],
                    ALU.mult, ALU.subtract), incs=(("V", 1),))
                stats_guard4[slot] = S.c["V"]
                S.do("vector", lambda e: e.tensor_single_scalar(
                    tmp_b[:, :], tmp_b[:, :], EPS, ALU.add), incs=(("V", 1),))
                v_var = S.c["V"]
                S.do("scalar", lambda e: e.activation(sig_sb[:, :], tmp_b[:, :], AF.Sqrt),
                     waits=(("V", v_var),), incs=(("A", 1),))
                a_sig = S.c["A"]
                S.do("vector", lambda e: e.reciprocal(rstd_sb[:, :], sig_sb[:, :]),
                     waits=(("A", a_sig),), incs=(("V", 1),))
                v_rstd = S.c["V"]
                for dc in range(DCH):
                    c0 = dc * QTOK + col0
                    S.do("gpsimd", lambda e, c0=c0: e.tensor_tensor(
                        tmp_g[:, :], x2[:, c0:c0 + 512], mean_sb[:, :], ALU.subtract),
                        waits=((("V", v_rstd),) if dc == 0 else ()), incs=(("G", 1),))
                    S.do("gpsimd", lambda e, c0=c0: e.tensor_tensor(
                        x2[:, c0:c0 + 512], tmp_g[:, :], rstd_sb[:, :], ALU.mult),
                        incs=(("G", 1),))
                g_guard4[0] = S.c["G"]
                return S.c["G"]

            outT_r = outT.rearrange("(dc p) q -> p dc q", p=128)

            def issue_w1(fc, extra=()):
                wslot = fc % 3
                w = list(extra)
                if w1_use[wslot] > 0:
                    w.append(("T", w1_use[wslot]))
                w1_dma[fc] = S.dma("sync", w1b[wslot][:, :], w1q_t[fc],
                                   f"W1{wslot}", waits=tuple(w))

            def issue_w2(i2, dc, extra=()):
                wslot = i2 % 3
                w = list(extra)
                if w2_use[wslot] > 0:
                    w.append(("T", w2_use[wslot]))
                w2_dma[i2] = S.dma("sync", w2b[wslot][:, :], w2q_t[dc],
                                   f"W2{wslot}", waits=tuple(w))

            ffn1_end = [0, 0]
            pending_ln = []

            def flush_ln():
                tcol, sdone = pending_ln.pop(0)
                for sg in range(2):
                    v_ln = ln_tail4(sg, tcol + sg * 512, (("T", sdone[sg]),))
                    c0 = tcol + sg * 512
                    S.do("gpsimd", lambda e, c0=c0: e.dma_start(
                        outT_r[:, :, c0:c0 + 512], x2_r[:, :, c0:c0 + 512]),
                        waits=(("G", v_ln), ("OST", S.c["OST"])) if S.c["OST"] > 0
                        else (("G", v_ln),),
                        incs=(("OST", 16),))

            for t in range(2):
                tcol = t * 1024
                x2bf_ready = cast_ready

                if t == 0:
                    w1_dma[0], w1_dma[1] = w1_pre
                    issue_w1(2)
                else:
                    for fc0 in range(3):
                        issue_w1(fc0)
                for fc in range(NFC):
                    wslot = fc % 3
                    for nt in range(2):
                        pslot = (fc * 2 + nt) % 2
                        for dc in range(DCH):
                            w = []
                            if dc == 0:
                                w = [(f"W1{fc % 3}", w1_dma[fc]), ("V", x2bf_ready)]
                                if relu_use[pslot] > 0:
                                    w.append(("A", relu_use[pslot]))
                            S.do("tensor",
                                 lambda e, dc=dc, nt=nt, wslot=wslot, pslot=pslot: e.matmul(
                                     f1ps[pslot][:, :],
                                     w1b[wslot][:, dc * 128:(dc + 1) * 128],
                                     x2bf[:, dc * 1024 + nt * 512: dc * 1024 + nt * 512 + 512],
                                     start=(dc == 0), stop=(dc == DCH - 1)),
                                 waits=tuple(w) if dc == 0 else (),
                                 incs=((("T", 1),) if dc == DCH - 1 else ()))
                        t_f1 = S.c["T"]
                        if nt == 1:
                            w1_use[wslot] = t_f1
                            if fc + 3 < NFC:
                                issue_w1(fc + 3)
                        S.do("scalar",
                             lambda e, fc=fc, nt=nt, pslot=pslot: e.activation(
                                 hT[:, fc * 1024 + nt * 512: fc * 1024 + nt * 512 + 512],
                                 f1ps[pslot][:, :], AF.Relu,
                                 bias=b1_sb[:, fc:fc + 1], scale=0.5),
                             waits=(("T", t_f1),), incs=(("A", 1),))
                        relu_use[pslot] = S.c["A"]
                a_relu_all = S.c["A"]
                ffn1_end[t] = S.c["T"]
                if t == 0:
                    S.dma("sync", mean4[:, :], meand[:, 1024:2048], "MRL",
                          waits=(("V", g_norm[0]),))
                    mrl_cnt[1] = S.dma("sync", rstd4[:, :], rstdd[:, 1024:2048],
                                       "MRL")
                    cast_ready = emit_norm_cast(1, ffn1_end[0])
                if t == 1:
                    flush_ln()

                stats_done4 = [0, 0]
                for nt in range(2):
                    if nt == 0:
                        for j in range(3):
                            issue_w2(j, j)
                    for dc in range(DCH):
                        gi = nt * DCH + dc
                        wslot = gi % 3
                        pslot = gi % 2
                        for fc in range(NFC):
                            w = []
                            if fc == 0:
                                w = [(f"W2{gi % 3}", w2_dma[gi]), ("A", a_relu_all)]
                                if f2_evac[pslot] > 0:
                                    w.append(("V", f2_evac[pslot]))
                            S.do("tensor",
                                 lambda e, fc=fc, nt=nt, wslot=wslot, pslot=pslot: e.matmul(
                                     f2ps[pslot][:, :],
                                     w2b[wslot][:, fc * 128:(fc + 1) * 128],
                                     hT[:, fc * 1024 + nt * 512: fc * 1024 + nt * 512 + 512],
                                     start=(fc == 0), stop=(fc == NFC - 1)),
                                 waits=tuple(w) if fc == 0 else (),
                                 incs=((("T", 1),) if fc == NFC - 1 else ()))
                        t_f2 = S.c["T"]
                        w2_use[wslot] = t_f2
                        if gi + 3 < 2 * DCH:
                            issue_w2(gi + 3, (gi + 3) % DCH)
                        wv = [("T", t_f2)]
                        if t == 1 and nt == 0 and dc == 0 and g_norm[1] > 0:
                            wv.append(("G", g_norm[1]))
                        S.do("vector",
                             lambda e, dc=dc, pslot=pslot: e.tensor_scalar(
                                 f_tmp[:, :], f2ps[pslot][:, :], 0.5,
                                 b2_sb[:, dc:dc + 1], ALU.mult, ALU.add),
                             waits=tuple(wv), incs=(("V", 1),))
                        c0 = dc * QTOK + tcol + nt * 512
                        S.do("vector", lambda e, c0=c0: e.tensor_tensor(
                            x2[:, c0:c0 + 512], f_tmp[:, :], x2[:, c0:c0 + 512], ALU.add),
                            incs=(("V", 1),))
                        f2_evac[pslot] = S.c["V"]
                        w_cast4 = [("V", f2_evac[pslot])]
                        if prev_stats4[0] > 0:
                            w_cast4.append(("T", prev_stats4[0]))
                        S.do("scalar", lambda e, c0=c0: e.copy(
                            t_bf[:, :], x2[:, c0:c0 + 512]),
                            waits=tuple(w_cast4), incs=(("A", 1),))
                        S.do("scalar", lambda e, c0=c0: e.activation(
                            t_sq[:, :], x2[:, c0:c0 + 512], AF.Square),
                            incs=(("A", 1),))
                        v_tiles = S.c["A"]
                        w_sum = [("A", v_tiles)]
                        if dc == 0 and stats_guard4[nt] > 0:
                            w_sum.append(("V", stats_guard4[nt]))
                        S.do("tensor", lambda e, dc=dc, nt=nt: e.matmul(
                            sum_ps[nt][:, :], ones_bf[:, :], t_bf[:, :],
                            start=(dc == 0), stop=(dc == DCH - 1)),
                            waits=tuple(w_sum))
                        S.do("tensor", lambda e, dc=dc, nt=nt: e.matmul(
                            sq_ps[nt][:, :], ones_bf[:, :], t_sq[:, :],
                            start=(dc == 0), stop=(dc == DCH - 1)),
                            incs=(("T", 1),))
                        prev_stats4[0] = S.c["T"]
                        stats_done4[nt] = S.c["T"]
                    if t == 1 and nt == 0:
                        v_ln = ln_tail4(0, tcol, (("T", stats_done4[0]),))
                        S.do("gpsimd", lambda e: e.dma_start(
                            outT_r[:, :, tcol:tcol + 512],
                            x2_r[:, :, tcol:tcol + 512]),
                            waits=(("G", v_ln), ("OST", S.c["OST"])) if S.c["OST"] > 0
                            else (("G", v_ln),),
                            incs=(("OST", 16),))

                if t == 0:
                    pending_ln.append((tcol, tuple(stats_done4)))
                else:
                    v_ln = ln_tail4(1, tcol + 512, (("T", stats_done4[1]),))
                    S.do("gpsimd", lambda e: e.dma_start(
                        outT_r[:, :, tcol + 512:tcol + 1024],
                        x2_r[:, :, tcol + 512:tcol + 1024]),
                        waits=(("G", v_ln), ("OST", S.c["OST"])),
                        incs=(("OST", 16),))

            S.do("gpsimd", None, waits=(("OST", S.c["OST"]),))
            S.flush()

    return nc


_NC = None


def _get_nc():
    global _NC
    if _NC is None:
        _NC = build_nc()
    return _NC


def _host_inputs(inputs):
    x = np.asarray(inputs["L0_attn_x"], dtype=np.float32)
    wq = np.asarray(inputs["L0_attn_Wq"], dtype=np.float32)
    wk = np.asarray(inputs["L0_attn_Wk"], dtype=np.float32)
    wv = np.asarray(inputs["L0_attn_Wv"], dtype=np.float32)
    wo = np.asarray(inputs["L0_attn_Wo"], dtype=np.float32)
    w1 = np.asarray(inputs["L0_ffn_W1"], dtype=np.float32)
    b1 = np.asarray(inputs["L0_ffn_b1"], dtype=np.float32)
    w2 = np.asarray(inputs["L0_ffn_W2"], dtype=np.float32)
    b2 = np.asarray(inputs["L0_ffn_b2"], dtype=np.float32)

    wqT = np.ascontiguousarray(wq.T)
    wkT = wk.T
    wkTd = np.ascontiguousarray(
        np.repeat(wkT.reshape(DM, NKV, HD), 2, axis=1).reshape(DM, 512))
    wvT = np.ascontiguousarray(wv.T)
    woT = np.ascontiguousarray(wo.T)
    w1T = np.ascontiguousarray(w1.T)
    w2T = np.ascontiguousarray(w2.T)
    b1c = np.ascontiguousarray(b1.reshape(NFC, 128).T)
    b2c = np.ascontiguousarray(b2.reshape(DCH, 128).T)

    u = np.arange(128)[:, None]
    qi = np.arange(128)[None, :]
    m0 = np.where(u <= qi, NEG, 0.0).astype(np.float32)
    m8 = np.where(u > qi, NEG, 0.0).astype(np.float32)
    trim = np.ascontiguousarray(
        np.concatenate([np.tile(m0, (1, 4)), np.tile(m8, (1, 4))],
                       axis=1)).astype(ml_dtypes.bfloat16)
    ident = np.eye(128, dtype=ml_dtypes.bfloat16)

    shared = dict(wqT=wqT, wkTd=wkTd, wvT=wvT, woT=woT, w1T=w1T, w2T=w2T,
                  b1c=b1c, b2c=b2c, trimask=trim, identin=ident)

    in_maps = []
    for c in range(8):
        b, h = c // 2, c % 2
        qstart = h * QTOK
        xc = np.zeros((CTX, DM), dtype=np.float32)
        lo = qstart - WINDOW
        if lo < 0:
            xc[-lo:, :] = x[b, 0:qstart + QTOK, :]
        else:
            xc[:, :] = x[b, lo:qstart + QTOK, :]
        xctxT = np.ascontiguousarray(xc.T)
        valid = np.ones((CTX, 128), dtype=ml_dtypes.bfloat16)
        if lo < 0:
            valid[0:-lo, :] = 0
        m = dict(shared)
        m["xctxT"] = xctxT
        m["validrep"] = valid
        in_maps.append(m)
    return in_maps


def kernel(**inputs):
    nc = _get_nc()
    in_maps = _host_inputs(inputs)
    res = run_bass_kernel_spmd(nc, in_maps, core_ids=list(range(8)))
    out = np.empty((B, S, DM), dtype=np.float32)
    for c in range(8):
        b, h = c // 2, c % 2
        out[b, h * QTOK:(h + 1) * QTOK, :] = res.results[c]["outT"].T
    return out
